# version 02
# baseline (speedup 1.0000x reference)
import math
import numpy as np
import ml_dtypes
import concourse.bass as bass
import concourse.mybir as mybir
from concourse.bass_core_rust import IndirectOffsetOnAxis
from concourse.bass_utils import run_bass_kernel_spmd

F32 = mybir.dt.float32
BF16 = mybir.dt.bfloat16
U32 = mybir.dt.uint32
AF = mybir.ActivationFunctionType
ALU = mybir.AluOpType
AX = mybir.AxisListType

D = 4096
NCHUNK = 32
TOK = 1024
SEQ = 2048
NH = 32
NKV = 8
HD = 128
NBLK = 1024
BLKSZ = 16
NPAGES = 512
INVSQ = 1.0 / math.sqrt(128.0)
CSTAB = 12.0
EPS = 1e-5

SCHED_A = [[8, 9, 10, 11, 0, 1, 2, 3], list(range(16))]
MSLOT_A = [
    [("scal", 0), ("scal", 1), ("scal", 2), ("scal", 3),
     ("tri", 0), ("tri", 1), ("tri", 2), ("tri", 3)],
    [None, None, None, None,
     ("tri", 0), ("tri", 1), ("tri", 2), ("tri", 3),
     None, None, None, None,
     ("scal", 4), ("scal", 5), ("scal", 6), ("scal", 7)],
]

import os
DEBUG_DUMP = int(os.environ.get("KDBG", "0"))
QPN = int(os.environ.get("KQPN", "16"))
KROPE = int(os.environ.get("KROPE", "0"))


class P:

    ENG = ("sync", "scalar", "vector", "tensor", "gpsimd")

    def __init__(self, nc):
        self.nc = nc
        self.ops = {e: [] for e in self.ENG}
        self.sems = {}
        self.cnt = {}
        self.lastw = {}

    def sem(self, name, ctx):
        self.sems[name] = ctx.enter_context(self.nc.semaphore(name))
        self.cnt[name] = 0
        return name

    def emit(self, eng, fn, waits=(), inc=None, n=1):
        wl = []
        flat = []
        for t in waits:
            if t is None:
                continue
            if isinstance(t, list):
                flat.extend(x for x in t if x is not None)
            else:
                flat.append(t)
        for t in flat:
            s, v = t
            k = (eng, s)
            if self.lastw.get(k, 0) < v:
                self.lastw[k] = v
                wl.append((s, v))
        tok = None
        if inc is not None:
            self.cnt[inc] += n
            tok = (inc, self.cnt[inc])
        self.ops[eng].append((wl, fn, inc, n))
        return tok

    def init_lanes(self, ctx, spec):
        self.lanes = {}
        self.lane_i = {}
        for eng, n in spec.items():
            self.lanes[eng] = [self.sem(f"dq_{eng}{i}", ctx) for i in range(n)]
            self.lane_i[eng] = 0

    def dma_emit(self, eng, fn, waits=()):
        lane = self.lanes[eng][self.lane_i[eng] % len(self.lanes[eng])]
        self.lane_i[eng] += 1
        base = self.cnt[lane]
        w = list(waits) + ([(lane, base)] if base else [])
        return self.emit(eng, fn, waits=w, inc=lane, n=16)

    def snap(self):
        return [(k, v) for (k, v) in self.cnt.items() if v > 0]

    def dma(self, eng, out, in_, waits=(), **kw):
        return self.dma_emit(
            eng, lambda e: e.dma_start(out=out, in_=in_, **kw), waits=waits)

    def build(self, ctx):
        block = ctx.enter_context(self.nc.Block())
        prog = self

        def runner(ename):
            def run(e):
                for (wl, fn, inc, n) in prog.ops[ename]:
                    for (s, v) in wl:
                        e.wait_ge(prog.sems[s], v)
                    ins = fn(e)
                    if inc is not None:
                        ins.then_inc(prog.sems[inc], n)
            return run

        block.sync(runner("sync"))
        block.scalar(runner("scalar"))
        block.vector(runner("vector"))
        block.tensor(runner("tensor"))
        block.gpsimd(runner("gpsimd"))


def cdiv(a, b):
    return (a + b - 1) // b


def build_program():
    nc = bass.Bass("TRN2", target_bir_lowering=False)
    dt = nc.dram_tensor
    xT = dt("xT", [D, SEQ], F32, kind="ExternalInput").ap()
    wqaT = dt("wqaT", [D, D], F32, kind="ExternalInput").ap()
    wkT = dt("wkT", [D, NKV * HD], F32, kind="ExternalInput").ap()
    wvT = dt("wvT", [D, NKV * HD], F32, kind="ExternalInput").ap()
    woaT = dt("woaT", [D, D], F32, kind="ExternalInput").ap()
    wqb = dt("wqb", [D, D], F32, kind="ExternalInput").ap()
    wob = dt("wob", [D, D], F32, kind="ExternalInput").ap()
    aT = dt("aT", [D, 128], F32, kind="ExternalInput").ap()
    bbig = dt("bbig", [128, D], F32, kind="ExternalInput").ap()
    wrT = dt("wrT", [D, 8], F32, kind="ExternalInput").ap()
    brD = dt("br", [8, 1], F32, kind="ExternalInput").ap()
    kvc = dt("kvc", [2, NKV, NBLK, BLKSZ * HD], F32, kind="ExternalInput").ap()
    btc = dt("btc", [128, 4], U32, kind="ExternalInput").ap()
    cosf = dt("cosf", [128, SEQ], BF16, kind="ExternalInput").ap()
    sinf = dt("sinf", [128, SEQ], BF16, kind="ExternalInput").ap()
    coso = dt("coso", [128, TOK], BF16, kind="ExternalInput").ap()
    sino = dt("sino", [128, TOK], BF16, kind="ExternalInput").ap()
    identF = dt("identF", [128, 128], F32, kind="ExternalInput").ap()
    identB = dt("identB", [128, 128], BF16, kind="ExternalInput").ap()
    rotT = dt("rotT", [128, 128], BF16, kind="ExternalInput").ap()
    e8 = dt("e8", [8, 128], BF16, kind="ExternalInput").ap()
    onesB = dt("onesB", [128, 1], BF16, kind="ExternalInput").ap()
    onesRF = dt("onesRF", [1, 128], F32, kind="ExternalInput").ap()
    masksD = dt("masks", [4, 128, 512], BF16, kind="ExternalInput").ap()
    mscalD = dt("mscal", [128, 8], F32, kind="ExternalInput").ap()
    outT = dt("outT", [D, TOK], F32, kind="ExternalOutput").ap()

    xn1 = dt("xn1", [D, SEQ], BF16, kind="Internal").ap()
    xn3 = dt("xn3", [D, TOK], BF16, kind="Internal").ap()
    kvg = dt("kvg", [2, NKV, NPAGES, BLKSZ * HD], BF16, kind="Internal").ap()
    qa_sp = dt("qa_sp", [NH, 128, TOK], BF16, kind="Internal").ap()
    qb_sp = dt("qb_sp", [NH, 128, TOK], BF16, kind="Internal").ap()
    ctx_sp = dt("ctx_sp", [NH, 128, TOK], BF16, kind="Internal").ap()

    dbg = dt("dbg", [D, TOK], F32, kind="ExternalOutput").ap() if DEBUG_DUMP else None

    from contextlib import ExitStack
    ctx = ExitStack()
    with ctx:
        E = ctx.enter_context
        p = P(nc)
        sSY = p.sem("sSY", ctx)
        sSC = p.sem("sSC", ctx)
        sGP = p.sem("sGP", ctx)
        sPE = p.sem("sPE", ctx)
        sAC = p.sem("sAC", ctx)
        sVE = p.sem("sVE", ctx)
        sPO = p.sem("sPO", ctx)
        p.init_lanes(ctx, {"sync": 8, "scalar": 6, "gpsimd": 6})

        PS0 = E(nc.psum_tensor("PS0", [128, SEQ], F32))
        PS1 = E(nc.psum_tensor("PS1", [128, SEQ], F32))

        TR = E(nc.sbuf_tensor("TR", [128, NCHUNK, TOK], F32))
        IDF = E(nc.sbuf_tensor("IDF", [128, 128], F32))
        IDB = E(nc.sbuf_tensor("IDB", [128, 128], BF16))
        ROT = E(nc.sbuf_tensor("ROT", [128, 128], BF16))
        E8S = E(nc.sbuf_tensor("E8S", [8, 128], BF16))
        ONE = E(nc.sbuf_tensor("ONE", [128, 1], BF16))
        ONEF = E(nc.sbuf_tensor("ONEF", [128, 1], F32))
        ONR = E(nc.sbuf_tensor("ONR", [1, 128], F32))
        BRS = E(nc.sbuf_tensor("BRS", [8, 1], F32))
        BTS = E(nc.sbuf_tensor("BTS", [128, 4], U32))

        tc_last = []
        for (dst, src) in ((IDF, identF), (IDB, identB), (ROT, rotT),
                           (E8S, e8), (ONE, onesB), (ONR, onesRF),
                           (BRS, brD), (BTS, btc)):
            tc_last.append(p.dma("sync", dst[:], src))
        t_const = tc_last
        EPSC = E(nc.sbuf_tensor("EPSC", [128, 1], F32))
        CSTB = E(nc.sbuf_tensor("CSTB", [128, 1], F32))
        p.emit("vector", lambda e: e.memset(EPSC[:], EPS), inc=sVE)
        p.emit("vector", lambda e: e.memset(CSTB[:], -CSTAB), inc=sVE)
        t_onef = p.emit("vector", lambda e: e.memset(ONEF[:], 1.0), inc=sVE)

        from contextlib import ExitStack as _ESg
        gstg_ctx = _ESg()
        GSTG = gstg_ctx.enter_context(
            nc.sbuf_tensor("GSTG", [128, 1, BLKSZ * HD], BF16))
        gjobs = [(kv, h, c) for kv in range(2) for h in range(NKV)
                 for c in range(4)]
        gst = {"i": 0, "free": [None, None], "done": None}

        def emit_gathers(njobs):
            for _ in range(njobs):
                i = gst["i"]
                if i >= len(gjobs):
                    return
                kv, h, c = gjobs[i]
                slot = 0
                stg = GSTG[:, 0, :]

                def g_fn(e, kv=kv, h=h, c=c, stg=stg):
                    flat = kvc.rearrange("a b c d -> (a b c) d")
                    return e.indirect_dma_start(
                        stg, None, flat,
                        IndirectOffsetOnAxis(BTS[:, c:c + 1], 0),
                        element_offset=(kv * NKV + h) * NBLK * (BLKSZ * HD))
                tg = p.dma_emit("gpsimd", g_fn,
                                waits=[t_const, gst["free"][slot]])
                ts = p.dma("sync", kvg[kv, h, c * 128:(c + 1) * 128, :], stg,
                           waits=[tg])
                gst["free"][slot] = ts
                gst["done"] = ts
                gst["i"] = i + 1

        emit_gathers(8)

        def ln_rows(N, t_stats, psA, psB, rows, reps):
            sum_ps = psA[0:1, :]
            sq_ps = psB[0:1, :]
            reps_ps = (psA, psB)
            """rows: 3 SBUF [1, N] slots (m, msq, var/r in place)."""
            m_row, msq_row, var_row = rows
            r_row = var_row
            t1 = p.emit("scalar", lambda e: e.activation(
                m_row, sum_ps, AF.Copy, scale=1.0 / D), waits=[t_stats], inc=sAC)
            t2 = p.emit("scalar", lambda e: e.activation(
                msq_row, sq_ps, AF.Copy, scale=1.0 / D), waits=[t_stats, t1],
                inc=sAC)
            t3 = p.emit("vector", lambda e: e.tensor_tensor(
                var_row, m_row, m_row, ALU.mult), waits=[t1], inc=sVE)
            t4 = p.emit("vector", lambda e: e.tensor_tensor(
                var_row, msq_row, var_row, ALU.subtract), waits=[t2, t3],
                inc=sVE)
            t5 = p.emit("scalar", lambda e: e.activation(
                r_row, var_row, AF.Ln, bias=EPSC[0:1, :]),
                waits=[t4, t_onef], inc=sAC)
            t6 = p.emit("scalar", lambda e: e.activation(
                r_row, r_row, AF.Exp, scale=-0.5), waits=[t5], inc=sAC)
            tm = tr = None
            for j in range(cdiv(N, 512)):
                sl = slice(j * 512, (j + 1) * 512)
                tm = p.emit("tensor", lambda e, sl=sl: e.matmul(
                    reps_ps[0][:, sl], ONR[:], m_row[:, sl], start=True,
                    stop=True), waits=[t1, t2, t_onef], inc=sPE)
                tr = p.emit("tensor", lambda e, sl=sl: e.matmul(
                    reps_ps[1][:, sl], ONR[:], r_row[:, sl], start=True,
                    stop=True), waits=[t6], inc=sPE)
            t_mrep = p.emit("scalar", lambda e: e.copy(reps[0], reps_ps[0]),
                            waits=[tm, tr], inc=sAC)
            t_rrep = p.emit("scalar", lambda e: e.copy(reps[1], reps_ps[1]),
                            waits=[tr], inc=sAC)
            return t_mrep, t_rrep

        from contextlib import ExitStack as _ES
        with _ES() as p1ctx:
            _e = p1ctx.enter_context
            xc = _e(nc.sbuf_tensor("p1_xc", [128, 2, SEQ], F32))
            xbf = _e(nc.sbuf_tensor("p1_xb", [128, 2, SEQ], BF16))
            rows1 = _e(nc.sbuf_tensor("p1_rows", [1, 3, SEQ], F32))
            reps1 = _e(nc.sbuf_tensor("p1_reps", [128, 2, SEQ], F32))
            xnb = _e(nc.sbuf_tensor("p1_xn", [128, 2, SEQ], BF16))
            mrep_ps = PS0[:, :]
            rrep_ps = PS1[:, :]
            sum_ps = mrep_ps
            sq_ps = rrep_ps

            ld_gate = [None, None]
            pe_gate = [None, None]
            t_stats = None
            trunk_init = {}
            for c in range(NCHUNK):
                sl = c % 2
                tl = p.dma("sync", xc[:, sl, :], xT[c * 128:(c + 1) * 128, :],
                           waits=[ld_gate[sl]])
                trunk_init[c] = p.emit("vector", lambda e, c=c, sl=sl: e.tensor_copy(
                    TR[:, c, :], xc[:, sl, 0:TOK]), waits=[tl], inc=sVE)
                tb = p.emit("vector", lambda e, sl=sl: e.tensor_copy(
                    xbf[:, sl, :], xc[:, sl, :]), waits=[tl, pe_gate[sl]],
                    inc=sVE)
                tsum = None
                for j in range(4):
                    s5 = slice(j * 512, (j + 1) * 512)
                    tsum = p.emit("tensor", lambda e, sl=sl, s5=s5, c=c: e.matmul(
                        sum_ps[0:1, s5], ONE[:], xbf[:, sl, s5],
                        start=(c == 0), stop=(c == NCHUNK - 1)),
                        waits=[tb, t_const], inc=sPE)
                tq = p.emit("scalar", lambda e, sl=sl: e.activation(
                    xbf[:, sl, :], xbf[:, sl, :], AF.Square),
                    waits=[tsum], inc=sAC)
                for j in range(4):
                    s5 = slice(j * 512, (j + 1) * 512)
                    t_stats = p.emit("tensor", lambda e, sl=sl, s5=s5, c=c: e.matmul(
                        sq_ps[0:1, s5], ONE[:], xbf[:, sl, s5],
                        start=(c == 0), stop=(c == NCHUNK - 1)),
                        waits=[tq], inc=sPE)
                ld_gate[sl] = tb
                pe_gate[sl] = t_stats
                if c % 8 == 3:
                    emit_gathers(2)

            rows = [rows1[:, i, :] for i in range(3)]
            t_mrep, t_rrep = ln_rows(SEQ, t_stats, mrep_ps, rrep_ps,
                                     rows, (reps1[:, 0, :], reps1[:, 1, :]))

            ld2 = list(ld_gate)
            st2 = [None, None]
            xn1_done = None
            for c in range(NCHUNK):
                sl = c % 2
                tl = p.dma("scalar", xc[:, sl, :], xT[c * 128:(c + 1) * 128, :],
                           waits=[ld2[sl]])
                td = p.emit("vector", lambda e, sl=sl: e.tensor_tensor(
                    xc[:, sl, :], xc[:, sl, :], reps1[:, 0, :], ALU.subtract),
                    waits=[tl, t_mrep, t_rrep], inc=sVE)
                tn = p.emit("vector", lambda e, sl=sl: e.tensor_tensor(
                    xnb[:, sl, :], xc[:, sl, :], reps1[:, 1, :], ALU.mult),
                    waits=[td, st2[sl]], inc=sVE)
                ld2[sl] = tn
                ts = p.dma("sync", xn1[c * 128:(c + 1) * 128, :], xnb[:, sl, :],
                           waits=[tn])
                st2[sl] = ts
                xn1_done = ts
                if c % 8 == 3:
                    emit_gathers(2)

        def q_production(tag, wT_dram, xn_dram, q_spill, t_xn_ready,
                         gather_hook=None, skip_rope=False, bar=None):
            spills = {}
            bar = bar or []
            from contextlib import ExitStack as _ES
            with _ES() as qpctx:
                _e = qpctx.enter_context
                ws = _e(nc.sbuf_tensor(f"{tag}_ws", [128, 2, NCHUNK, 256], BF16))
                xcq = _e(nc.sbuf_tensor(f"{tag}_xc", [128, 2, TOK], BF16))
                qbr = _e(nc.sbuf_tensor(f"{tag}_qb", [128, TOK], BF16))
                t1s = _e(nc.sbuf_tensor(f"{tag}_t1", [128, TOK], F32))
                t2s = _e(nc.sbuf_tensor(f"{tag}_t2", [128, TOK], F32))
                qo = _e(nc.sbuf_tensor(f"{tag}_qo", [128, 2, TOK], BF16))
                cs = _e(nc.sbuf_tensor(f"{tag}_cs", [128, 2, TOK], BF16))
                q0 = PS0[:, 0:TOK]
                q1 = PS0[:, TOK:SEQ]
                rp = PS1[:, 0:TOK]
                tcs = p.dma("sync", cs[:, 0, :], coso, waits=[bar])
                tsn = p.dma("sync", cs[:, 1, :], sino, waits=[bar])
                qps = [q0, q1]
                wfree = [bar, bar]
                xfree = [bar, bar]
                qofree = [bar, bar]
                rope_done = {(-1, 0): bar, (-1, 1): bar}
                last_rot_read = bar
                last_qbr_read = bar
                last_tv3 = bar
                for ps in range(QPN):
                    slw = ps % 2
                    if gather_hook:
                        gather_hook(3)
                    tw = p.dma("gpsimd", ws[:, slw, :, :],
                               wT_dram[:, ps * 256:(ps + 1) * 256].rearrange(
                                   "(c p) w -> p c w", p=128),
                               waits=[wfree[slw]])
                    mm_tok = [None, None]
                    for c in range(NCHUNK):
                        slx = c % 2
                        tl = p.dma("sync", xcq[:, slx, :],
                                   xn_dram[c * 128:(c + 1) * 128, 0:TOK],
                                   waits=[t_xn_ready, xfree[slx]])
                        for hh in range(2):
                            wmm = [tl, tw, rope_done.get((ps - 1, hh))]
                            for j in range(2):
                                s5 = slice(j * 512, (j + 1) * 512)
                                mm_tok[hh] = p.emit(
                                    "tensor",
                                    lambda e, hh=hh, c=c, s5=s5, slx=slx, slw=slw: e.matmul(
                                        qps[hh][:, s5],
                                        ws[:, slw, c, hh * 128:(hh + 1) * 128],
                                        xcq[:, slx, s5],
                                        start=(c == 0), stop=(c == NCHUNK - 1)),
                                    waits=wmm, inc=sPE)
                        xfree[slx] = mm_tok[1]
                    wfree[slw] = mm_tok[1]
                    for hh in range(2):
                        h = ps * 2 + hh
                        slq = hh
                        if KROPE in (2, 3):
                            te = p.emit("scalar", lambda e, hh=hh: e.copy(
                                qbr[:], qps[hh]),
                                waits=[mm_tok[hh], last_rot_read,
                                       last_qbr_read], inc=sAC)
                            trm = te
                            if KROPE == 3:
                                for j in range(2):
                                    s5 = slice(j * 512, (j + 1) * 512)
                                    trm = p.emit("tensor", lambda e, s5=s5: e.matmul(
                                        rp[:, s5], ROT[:], qbr[:, s5],
                                        start=True, stop=True),
                                        waits=[te, last_rot_read], inc=sPE)
                            last_qbr_read = trm
                            tcp = p.emit("scalar", lambda e, slq=slq: e.copy(
                                qo[:, slq, :], qbr[:]),
                                waits=[trm, qofree[slq]], inc=sAC)
                            rope_done[(ps, hh)] = tcp
                            last_rot_read = tcp
                            tsp = p.dma("scalar", q_spill[h], qo[:, slq, :],
                                        waits=[tcp])
                            qofree[slq] = tsp
                            spills[h] = tsp
                            continue
                        if skip_rope:
                            te = p.emit("scalar", lambda e, hh=hh, slq=slq: e.copy(
                                qo[:, slq, :], qps[hh]),
                                waits=[mm_tok[hh], qofree[slq]], inc=sAC)
                            rope_done[(ps, hh)] = te
                            tsp = p.dma("scalar", q_spill[h], qo[:, slq, :],
                                        waits=[te])
                            qofree[slq] = tsp
                            spills[h] = tsp
                            continue
                        te = p.emit("scalar", lambda e, hh=hh: e.copy(
                            qbr[:], qps[hh]),
                            waits=[mm_tok[hh], last_rot_read, last_qbr_read],
                            inc=sAC)
                        trm = None
                        if DEBUG_DUMP == 122:
                            trm = te
                            last_qbr_read = te
                        else:
                            for j in range(2):
                                s5 = slice(j * 512, (j + 1) * 512)
                                trm = p.emit("tensor", lambda e, s5=s5: e.matmul(
                                    rp[:, s5], ROT[:], qbr[:, s5], start=True,
                                    stop=True), waits=[te, last_rot_read],
                                    inc=sPE)
                            last_qbr_read = trm
                        tv1 = p.emit("vector", lambda e: e.tensor_tensor(
                            t1s[:], qbr[:], cs[:, 0, :], ALU.mult),
                            waits=[te, tcs, last_tv3], inc=sVE)
                        tv2 = p.emit("vector", lambda e: e.tensor_tensor(
                            t2s[:], rp, cs[:, 1, :], ALU.mult),
                            waits=[trm, tsn, tv1], inc=sVE)
                        tv3 = p.emit("vector", lambda e, slq=slq: e.tensor_tensor(
                            qo[:, slq, :], t1s[:], t2s[:], ALU.add),
                            waits=[tv2, qofree[slq]], inc=sVE)
                        rope_done[(ps, hh)] = tv3
                        last_tv3 = tv3
                        last_rot_read = tv2
                        tsp = p.dma("scalar", q_spill[h], qo[:, slq, :],
                                    waits=[tv3])
                        qofree[slq] = tsp
                        spills[h] = tsp
            return spills

        def flash(s, qheads, kt_fn, v_fn, q_spill, q_ready, t_k, t_v,
                  halves, msk, ctx_dst, ctx_ready, t_msk=None):
            for qh in qheads:
                tql = p.dma("sync", s["qbuf"][:, qh % 2, :], q_spill[qh],
                            waits=[q_ready[qh], s["qfree"][qh % 2]])
                ctx_tok = None
                denom_mm = []
                for half, (cols, steps) in enumerate(halves):
                    ncol = cols.stop - cols.start
                    dacc = s["dacc"][:, half, 0:ncol]
                    tz = p.emit("vector", lambda e, dacc=dacc: e.memset(
                        dacc, 0.0), waits=[s["dacc_free"][half]], inc=sVE)
                    prev_d = tz
                    for si, (tile, slot) in enumerate(steps):
                        pp = s["pi"] % 2
                        s["pi"] += 1
                        scb = s["sc_ps"][pp]
                        nmm = cdiv(ncol, 512)
                        tmm = None
                        for j in range(nmm):
                            s5 = slice(j * 512, min((j + 1) * 512, ncol))
                            qsl = slice(cols.start + s5.start,
                                        cols.start + s5.stop)
                            tmm = p.emit(
                                "tensor",
                                lambda e, tile=tile, s5=s5, qsl=qsl, scb=scb, qh=qh: e.matmul(
                                    scb[:, s5], kt_fn(tile),
                                    s["qbuf"][:, qh % 2, qsl],
                                    start=True, stop=True),
                                waits=[tql, t_k, s["sc_free"][pp]], inc=sPE)
                        pb = s["p_sb"][:, pp, 0:ncol]
                        tex = p.emit("scalar",
                                     lambda e, pb=pb, scb=scb, ncol=ncol: e.activation(
                                         pb, scb[:, 0:ncol], AF.Exp,
                                         bias=CSTB[:], scale=INVSQ),
                                     waits=[tmm, t_onef] + s["p_free"][pp],
                                     inc=sAC)
                        s["sc_free"][pp] = tex
                        tcons = tex
                        if slot is not None:
                            kind, j = slot
                            if kind == "tri":
                                tcons = p.emit(
                                    "vector",
                                    lambda e, pb=pb, j=j: e.tensor_tensor(
                                        pb, pb, msk[0][:, j, :], ALU.mult),
                                    waits=[tex, t_msk], inc=sVE)
                            else:
                                tcons = p.emit(
                                    "vector",
                                    lambda e, pb=pb, j=j: e.tensor_scalar(
                                        pb, pb, msk[1][:, j:j + 1], None,
                                        ALU.mult),
                                    waits=[tex, t_msk], inc=sVE)
                        td = p.emit("vector", lambda e, dacc=dacc, pb=pb: e.tensor_tensor(
                            dacc, dacc, pb, ALU.add), waits=[tcons, prev_d],
                            inc=sVE)
                        prev_d = td
                        tav = None
                        for j in range(nmm):
                            s5 = slice(j * 512, min((j + 1) * 512, ncol))
                            csl = slice(cols.start + s5.start,
                                        cols.start + s5.stop)
                            tav = p.emit(
                                "tensor",
                                lambda e, tile=tile, s5=s5, csl=csl, pb=pb, si=si, nst=len(steps): e.matmul(
                                    s["ctx_ps"][:, csl], v_fn(tile), pb[:, s5],
                                    start=(si == 0), stop=(si == nst - 1)),
                                waits=[tcons, t_v, s["ctx_gate"]], inc=sPE)
                        s["p_free"][pp] = [tav, td]
                        ctx_tok = tav
                    for j in range(cdiv(ncol, 512)):
                        s5 = slice(cols.start + j * 512,
                                   min(cols.start + (j + 1) * 512, cols.stop))
                        dsl = slice(j * 512, min((j + 1) * 512, ncol))
                        tdm = p.emit("tensor", lambda e, s5=s5, dacc=dacc, dsl=dsl: e.matmul(
                            s["dn_ps"][0:1, s5], ONEF[:], dacc[:, dsl],
                            start=True, stop=True),
                            waits=[prev_d, t_onef, s["dn_free"]], inc=sPE)
                        denom_mm.append(tdm)
                    s["dacc_free"][half] = denom_mm[-1]
                trc = p.emit("vector", lambda e: e.reciprocal(
                    s["dn_row"], s["dn_ps"][0:1, :]),
                    waits=denom_mm + [s["dr_sb_free"]], inc=sVE)
                trr = None
                for j in range(2):
                    s5 = slice(j * 512, (j + 1) * 512)
                    trr = p.emit("tensor", lambda e, s5=s5: e.matmul(
                        s["dn_ps"][:, s5], ONR[:], s["dn_row"][:, s5],
                        start=True, stop=True), waits=[trc], inc=sPE)
                tre = p.emit("scalar", lambda e: e.copy(
                    s["dr_sb"][:], s["dn_ps"]),
                    waits=[trr, s["dr_sb_free"]], inc=sAC)
                s["dn_free"] = tre
                tnm = p.emit("vector", lambda e, qh=qh: e.tensor_tensor(
                    s["ctx_sb"][:, qh % 2, :], s["ctx_ps"], s["dr_sb"][:],
                    ALU.mult), waits=[tre, ctx_tok, s["co_free"][qh % 2]],
                    inc=sVE)
                s["dr_sb_free"] = tnm
                s["ctx_gate"] = tnm
                tsp = p.dma("scalar", ctx_dst[qh], s["ctx_sb"][:, qh % 2, :],
                            waits=[tnm])
                s["co_free"][qh % 2] = tsp
                s["qfree"][qh % 2] = tsp
                ctx_ready[qh] = tsp
            return s["ctx_gate"]

        def wo_residual(tag, wo_dram, ctx_ready, trunk_gate, bar=None):
            res_tok = {}
            bar = bar or []
            from contextlib import ExitStack as _ES
            with _ES() as woctx:
                _e = woctx.enter_context
                wos = _e(nc.sbuf_tensor(f"{tag}_wo", [128, NCHUNK, 512], BF16))
                cxs = _e(nc.sbuf_tensor(f"{tag}_cx", [128, 2, TOK], BF16))
                yps = [PS0[:, 0:TOK], PS0[:, TOK:SEQ], PS1[:, 0:TOK],
                       PS1[:, TOK:SEQ]]
                wfree = bar
                psum_gate = bar
                cfree = [bar, bar]
                for grp in range(8):
                    tw = p.dma("gpsimd", wos[:, :, :],
                               wo_dram[:, grp * 512:(grp + 1) * 512].rearrange(
                                   "(c p) w -> p c w", p=128),
                               waits=[wfree])
                    tmm = None
                    for h in range(NH):
                        slc = h % 2
                        tc = p.dma("sync", cxs[:, slc, :], ctx_sp[h],
                                   waits=[ctx_ready.get(h), cfree[slc]])
                        for j in range(4):
                            for q5 in range(2):
                                s5 = slice(q5 * 512, (q5 + 1) * 512)
                                tmm = p.emit(
                                    "tensor",
                                    lambda e, h=h, j=j, s5=s5, slc=slc: e.matmul(
                                        yps[j][:, s5],
                                        wos[:, h, j * 128:(j + 1) * 128],
                                        cxs[:, slc, s5],
                                        start=(h == 0), stop=(h == NH - 1)),
                                    waits=[tw, tc, psum_gate], inc=sPE)
                        cfree[slc] = tmm
                    wfree = tmm
                    tev = None
                    for j in range(4):
                        oc = grp * 4 + j
                        tev = p.emit("vector", lambda e, j=j, oc=oc: e.tensor_tensor(
                            TR[:, oc, :], TR[:, oc, :], yps[j], ALU.add),
                            waits=[tmm, trunk_gate.get(oc)], inc=sVE)
                        res_tok[oc] = tev
                    psum_gate = tev
                    wfree = tev
            return res_tok

        def ln_trunk(tag, trunk_gate, consume, pool_engine="gpsimd",
                     bar=None):
            bar = bar or []
            from contextlib import ExitStack as _ES
            with _ES() as lnctx:
                _e = lnctx.enter_context
                xb2 = _e(nc.sbuf_tensor(f"{tag}_b", [128, 2, TOK], BF16))
                xq2 = _e(nc.sbuf_tensor(f"{tag}_q", [128, 2, TOK], BF16))
                rows2 = _e(nc.sbuf_tensor(f"{tag}_rows", [1, 3, TOK], F32))
                reps2 = _e(nc.sbuf_tensor(f"{tag}_reps", [128, 2, TOK], F32))
                xn2 = _e(nc.sbuf_tensor(f"{tag}_xn", [128, 2, TOK], BF16))
                dd2 = _e(nc.sbuf_tensor(f"{tag}_d", [128, 2, TOK], F32))
                mr2 = PS0[:, TOK:SEQ]
                rr2 = PS1[:, TOK:SEQ]
                pe_free = [bar, bar]
                t_stats = None
                for c in range(NCHUNK):
                    sl = c % 2
                    tb = p.emit(pool_engine, lambda e, c=c, sl=sl: e.tensor_copy(
                        xb2[:, sl, :], TR[:, c, :]),
                        waits=[trunk_gate.get(c), pe_free[sl]], inc=sPO)
                    tq = p.emit(pool_engine, lambda e, sl=sl: e.tensor_tensor(
                        xq2[:, sl, :], xb2[:, sl, :], xb2[:, sl, :], ALU.mult),
                        waits=[tb], inc=sPO)
                    for j in range(2):
                        s5 = slice(j * 512, (j + 1) * 512)
                        p.emit("tensor", lambda e, sl=sl, s5=s5, c=c: e.matmul(
                            mr2[0:1, s5], ONE[:], xb2[:, sl, s5],
                            start=(c == 0), stop=(c == NCHUNK - 1)),
                            waits=[tb, t_const], inc=sPE)
                        t_stats = p.emit("tensor", lambda e, sl=sl, s5=s5, c=c: e.matmul(
                            rr2[0:1, s5], ONE[:], xq2[:, sl, s5],
                            start=(c == 0), stop=(c == NCHUNK - 1)),
                            waits=[tq], inc=sPE)
                    pe_free[sl] = t_stats
                rows = [rows2[:, i, :] for i in range(3)]
                t_mrep, t_rrep = ln_rows(TOK, t_stats, mr2, rr2, rows,
                                         (reps2[:, 0, :], reps2[:, 1, :]))
                cons_tok = [bar, bar]
                for c in range(NCHUNK):
                    sl = c % 2
                    td = p.emit("vector", lambda e, c=c, sl=sl: e.tensor_tensor(
                        dd2[:, sl, :], TR[:, c, :], reps2[:, 0, :],
                        ALU.subtract),
                        waits=[trunk_gate.get(c), t_mrep, t_rrep, cons_tok[sl]],
                        inc=sVE)
                    tn = p.emit("vector", lambda e, sl=sl: e.tensor_tensor(
                        xn2[:, sl, :], dd2[:, sl, :], reps2[:, 1, :], ALU.mult),
                        waits=[td], inc=sVE)
                    cons_tok[sl] = consume(c, xn2[:, sl, :], tn)

        def dump_and_finish(gate):
            last = None
            tgt = dbg if DEBUG_DUMP else outT
            for c in range(NCHUNK):
                last = p.dma("sync", tgt[c * 128:(c + 1) * 128, :], TR[:, c, :],
                             waits=[gate.get(c)])
            tail = [last]
            for eng, lns in p.lanes.items():
                for ln in lns:
                    if p.cnt[ln]:
                        tail.append((ln, p.cnt[ln]))
            p.emit("sync", lambda e: e.nop(), waits=tail)

        if DEBUG_DUMP == 11:
            emit_gathers(len(gjobs))
            gstg_ctx.close()
            p.emit("vector", lambda e: e.engine_nop(),
                   waits=[xn1_done, gst["done"]], inc=sVE)
            dump_and_finish(trunk_init)
            p.build(ctx)
            return nc

        bar_p1 = p.snap()
        qa_ready = q_production("qa", wqaT, xn1, qa_sp, xn1_done,
                                gather_hook=emit_gathers,
                                skip_rope=(DEBUG_DUMP == 121), bar=bar_p1)
        emit_gathers(len(gjobs))
        gstg_ctx.close()
        if DEBUG_DUMP in (12, 121, 122):
            p.emit("vector", lambda e: e.engine_nop(),
                   waits=[qa_ready[QPN * 2 - 1], gst["done"]], inc=sVE)
            dump_and_finish(trunk_init)
            p.build(ctx)
            return nc

        bar3 = p.snap()
        ctxa_ready = {}
        halves_a = []
        for half in range(2):
            cols = slice(half * 512, half * 512 + 512)
            steps = [(SCHED_A[half][s], MSLOT_A[half][s])
                     for s in range(len(SCHED_A[half]))]
            halves_a.append((cols, steps))

        from contextlib import ExitStack as _ES
        with _ES() as p3ctx:
            _e = p3ctx.enter_context
            MSK = _e(nc.sbuf_tensor("a_msk", [128, 4, 512], BF16))
            MS = _e(nc.sbuf_tensor("a_ms", [128, 8], F32))
            CSF = _e(nc.sbuf_tensor("a_cs", [128, 2, SEQ], BF16))
            WKV = _e(nc.sbuf_tensor("a_wk", [128, NCHUNK, 256], BF16))
            XF = _e(nc.sbuf_tensor("a_xf", [128, 2, SEQ], BF16))
            KBR = _e(nc.sbuf_tensor("a_kb", [128, SEQ], BF16))
            VTD = KBR
            AT1 = _e(nc.sbuf_tensor("a_t1", [128, SEQ], BF16))
            AT2 = _e(nc.sbuf_tensor("a_t2", [128, SEQ], BF16))
            KT = _e(nc.sbuf_tensor("a_kt", [128, SEQ], BF16))
            VG = _e(nc.sbuf_tensor("a_vg", [128, 16, 128], BF16))
            AQB = _e(nc.sbuf_tensor("a_qb", [128, 2, TOK], BF16))
            APB = _e(nc.sbuf_tensor("a_p", [128, 2, 512], BF16))
            ADA = _e(nc.sbuf_tensor("a_da", [128, 2, 512], F32))
            ADRS = _e(nc.sbuf_tensor("a_drs", [128, TOK], F32))
            ADNR = ADRS[0:1, :]
            ACX = _e(nc.sbuf_tensor("a_cx", [128, 2, TOK], BF16))

            p.dma("sync", MSK[:], masksD.rearrange("m p q -> p m q"),
                  waits=[gst["done"], bar3])
            tmsk = p.dma("sync", MS[:], mscalD, waits=[bar3])
            tcsf = p.dma("sync", CSF[:, 0, :], cosf, waits=[bar3])
            tsnf = p.dma("sync", CSF[:, 1, :], sinf, waits=[bar3])

            fs = {"qbuf": AQB, "qfree": [bar3, bar3], "sc_ps": None,
                  "sc_free": [bar3, bar3], "p_sb": APB,
                  "p_free": [[bar3], [bar3]], "pi": 0, "dacc": ADA,
                  "dacc_free": [bar3, bar3], "dn_ps": None, "dn_row": ADNR,
                  "dn_free": bar3, "dr_sb": ADRS, "dr_sb_free": bar3,
                  "ctx_ps": None, "ctx_sb": ACX, "co_free": [bar3, bar3],
                  "ctx_gate": bar3}

            xf_free = [bar3, bar3]
            wkv_free = bar3
            kv_sbuf_gate = bar3
            APS = PS0[:, :]
            BPS = PS1[:, :]
            for g in range(NKV):
                twk = p.dma("gpsimd", WKV[:, :, 0:128],
                            wkT[:, g * 128:(g + 1) * 128].rearrange(
                                "(c p) w -> p c w", p=128),
                            waits=[wkv_free])
                twv = p.dma("gpsimd", WKV[:, :, 128:256],
                            wvT[:, g * 128:(g + 1) * 128].rearrange(
                                "(c p) w -> p c w", p=128),
                            waits=[wkv_free])
                tmmk = tmmv = None
                for c in range(NCHUNK):
                    sl = c % 2
                    tl = p.dma("sync", XF[:, sl, :],
                               xn1[c * 128:(c + 1) * 128, :],
                               waits=[xn1_done, xf_free[sl]])
                    for j in range(4):
                        s5 = slice(j * 512, (j + 1) * 512)
                        tmmk = p.emit("tensor", lambda e, c=c, sl=sl, s5=s5: e.matmul(
                            APS[:, s5], WKV[:, c, 0:128], XF[:, sl, s5],
                            start=(c == 0), stop=(c == NCHUNK - 1)),
                            waits=[tl, twk, twv, kv_sbuf_gate], inc=sPE)
                        tmmv = p.emit("tensor", lambda e, c=c, sl=sl, s5=s5: e.matmul(
                            BPS[:, s5], WKV[:, c, 128:256], XF[:, sl, s5],
                            start=(c == 0), stop=(c == NCHUNK - 1)),
                            waits=[tl], inc=sPE)
                    xf_free[sl] = tmmv
                wkv_free = tmmv
                tke = p.emit("scalar", lambda e: e.copy(KBR[:], APS[:]),
                             waits=[tmmk, kv_sbuf_gate], inc=sAC)
                tv1 = p.emit("vector", lambda e: e.tensor_tensor(
                    AT1[:], KBR[:], CSF[:, 0, :], ALU.mult),
                    waits=[tke, tcsf, kv_sbuf_gate], inc=sVE)
                trm = None
                for j in range(4):
                    s5 = slice(j * 512, (j + 1) * 512)
                    trm = p.emit("tensor", lambda e, s5=s5: e.matmul(
                        APS[:, s5], ROT[:], KBR[:, s5], start=True,
                        stop=True), waits=[tke, tv1], inc=sPE)
                tve = p.emit("scalar", lambda e: e.copy(VTD[:], BPS[:]),
                             waits=[tmmv, trm, kv_sbuf_gate], inc=sAC)
                tv2 = p.emit("vector", lambda e: e.tensor_tensor(
                    AT2[:], APS[:], CSF[:, 1, :], ALU.mult),
                    waits=[trm, tsnf], inc=sVE)
                tkt = p.emit("vector", lambda e: e.tensor_tensor(
                    KT[:], AT1[:], AT2[:], ALU.add),
                    waits=[tv2, kv_sbuf_gate], inc=sVE)
                tvg = []
                for j in range(16):
                    tvg.append(p.dma_emit(
                        "sync",
                        lambda e, j=j: e.dma_start_transpose(
                            VG[:, j, :], VTD[:, j * 128:(j + 1) * 128]),
                        waits=[tve, kv_sbuf_gate]))
                fs["sc_ps"] = [APS[:, 0:512], APS[:, 512:1024]]
                fs["ctx_ps"] = BPS[:, 0:TOK]
                fs["dn_ps"] = APS[:, TOK:SEQ]
                kv_sbuf_gate = flash(
                    fs, [4 * g + i for i in range(4)],
                    lambda t: KT[:, t * 128:(t + 1) * 128],
                    lambda t: VG[:, t, :], qa_sp, qa_ready, [tkt, tke, tv2],
                    tvg, halves_a, (MSK, MS), ctx_sp, ctxa_ready, t_msk=tmsk)

        if DEBUG_DUMP == 13:
            p.emit("vector", lambda e: e.engine_nop(),
                   waits=[ctxa_ready[NH - 1]], inc=sVE)
            dump_and_finish(trunk_init)
            p.build(ctx)
            return nc
        bar4 = p.snap()
        res1 = wo_residual("woa", woaT, ctxa_ready, {}, bar=bar4)

        if DEBUG_DUMP == 1:
            dump_and_finish(res1)
            p.build(ctx)
            return nc

        ATS = E(nc.sbuf_tensor("ATS", [128, NCHUNK, 128], BF16))
        BBS = E(nc.sbuf_tensor("BBS", [128, NCHUNK, 128], BF16))
        WRS = E(nc.sbuf_tensor("WRS", [128, NCHUNK, 8], BF16))
        bar_ff = p.snap()
        t_at = p.dma("gpsimd", ATS[:], aT.rearrange("(c p) w -> p c w", p=128),
                     waits=[bar_ff])
        t_bb = p.dma("gpsimd", BBS[:],
                     bbig.rearrange("p (c w) -> p c w", w=128),
                     waits=[bar_ff])
        t_wr = p.dma("gpsimd", WRS[:], wrT.rearrange("(c p) w -> p c w", p=128),
                     waits=[bar_ff])

        def ffn(tag, trunk_gate, bar=None):
            res = {}
            bar = bar or []
            from contextlib import ExitStack as _ES
            with _ES() as ffctx:
                _e = ffctx.enter_context
                RR = _e(nc.sbuf_tensor(f"{tag}_rr", [8, TOK], F32))
                RTOK = _e(nc.sbuf_tensor(f"{tag}_rt", [128, 8, 8], F32))
                MX = _e(nc.sbuf_tensor(f"{tag}_mx", [128, 8, 1], F32))
                EXT = _e(nc.sbuf_tensor(f"{tag}_ex", [128, 8, 8], F32))
                SM = _e(nc.sbuf_tensor(f"{tag}_sm", [128, 8, 1], F32))
                STOK = _e(nc.sbuf_tensor(f"{tag}_sb", [128, 8, 8], F32))
                STT = _e(nc.sbuf_tensor(f"{tag}_st", [8, TOK], BF16))
                SREP = _e(nc.sbuf_tensor(f"{tag}_sr", [128, TOK], F32))
                LOWB = _e(nc.sbuf_tensor(f"{tag}_lo", [128, TOK], BF16))
                psA = PS0[:, 0:TOK]
                psC = PS1[:, 0:TOK]

                fin_tok = []

                def consume(c, xn_ap, tn):
                    last = None
                    for j in range(2):
                        s5 = slice(j * 512, (j + 1) * 512)
                        p.emit("tensor", lambda e, c=c, s5=s5, xn_ap=xn_ap: e.matmul(
                            psC[0:8, s5], WRS[:, c, :], xn_ap[:, s5],
                            start=(c == 0), stop=(c == NCHUNK - 1)),
                            waits=[tn, t_wr, bar], inc=sPE)
                        last = p.emit("tensor", lambda e, c=c, s5=s5, xn_ap=xn_ap: e.matmul(
                            psA[:, s5], ATS[:, c, :], xn_ap[:, s5],
                            start=(c == 0), stop=(c == NCHUNK - 1)),
                            waits=[t_at], inc=sPE)
                    if c == NCHUNK - 1:
                        fin_tok.append(last)
                    return last

                ln_trunk(tag + "l", trunk_gate, consume, bar=bar)
                psB = PS0[:, TOK:SEQ]
                psD = PS1[:, TOK:SEQ]
                t_rl = fin_tok[0]
                t_rr = p.emit("vector", lambda e: e.tensor_scalar(
                    RR[:], psC[0:8, :], BRS[:], None, ALU.add),
                    waits=[t_rl, t_const], inc=sVE)
                tp_prev = None
                tp_toks = []
                for j in range(8):
                    tt = p.emit("tensor", lambda e, j=j: e.transpose(
                        psB[:, j * 8:(j + 1) * 8],
                        RR[:, j * 128:(j + 1) * 128], IDF[0:8, 0:8]),
                        waits=[t_rr, t_const, tp_prev], inc=sPE)
                    te = p.emit("scalar", lambda e, j=j: e.copy(
                        RTOK[:, j, :], psB[:, j * 8:(j + 1) * 8]), waits=[tt],
                        inc=sAC)
                    tp_prev = te
                    tp_toks.append(te)
                sm_toks = []
                for j in range(8):
                    tmx = p.emit("vector", lambda e, j=j: e.tensor_reduce(
                        MX[:, j, :], RTOK[:, j, :], AX.X, ALU.max, negate=True),
                        waits=[tp_toks[j]], inc=sVE)
                    tex = p.emit("scalar", lambda e, j=j: e.activation(
                        EXT[:, j, :], RTOK[:, j, :], AF.Exp, bias=MX[:, j, :]),
                        waits=[tmx], inc=sAC)
                    tsm = p.emit("vector", lambda e, j=j: e.tensor_reduce(
                        SM[:, j, :], EXT[:, j, :], AX.X, ALU.add),
                        waits=[tex], inc=sVE)
                    trc = p.emit("vector", lambda e, j=j: e.reciprocal(
                        SM[:, j, :], SM[:, j, :]), waits=[tsm], inc=sVE)
                    tmu = p.emit("vector", lambda e, j=j: e.tensor_scalar(
                        STOK[:, j, :], EXT[:, j, :], SM[:, j, :], None,
                        ALU.mult), waits=[trc], inc=sVE)
                    sm_toks.append(tmu)
                tb_prev = tp_prev
                tst = None
                for j in range(8):
                    tt = p.emit("tensor", lambda e, j=j: e.transpose(
                        psB[0:8, 512:640], STOK[:, j, :], IDF[:]),
                        waits=[sm_toks[j], tb_prev], inc=sPE)
                    tst = p.emit("scalar", lambda e, j=j: e.copy(
                        STT[:, j * 128:(j + 1) * 128], psB[0:8, 512:640]),
                        waits=[tt], inc=sAC)
                    tb_prev = tst
                tsr = None
                for j in range(2):
                    s5 = slice(j * 512, (j + 1) * 512)
                    tsr = p.emit("tensor", lambda e, s5=s5: e.matmul(
                        psB[:, s5], E8S[:], STT[:, s5], start=True, stop=True),
                        waits=[tst, t_const], inc=sPE)
                tse = p.emit("scalar", lambda e: e.copy(SREP[:], psB),
                             waits=[tsr], inc=sAC)
                tlo = p.emit("vector", lambda e: e.tensor_tensor(
                    LOWB[:], psA, SREP[:], ALU.mult),
                    waits=[fin_tok[0], tse], inc=sVE)
                pps = [psC, psD]
                pp_free = [t_rr, None]
                for oc in range(NCHUNK):
                    sl = oc % 2
                    tmm = None
                    for j in range(2):
                        s5 = slice(j * 512, (j + 1) * 512)
                        tmm = p.emit("tensor", lambda e, oc=oc, s5=s5, sl=sl: e.matmul(
                            pps[sl][:, s5], BBS[:, oc, :], LOWB[:, s5],
                            start=True, stop=True),
                            waits=[tlo, t_bb, pp_free[sl]], inc=sPE)
                    tev = p.emit("vector", lambda e, oc=oc, sl=sl: e.tensor_tensor(
                        TR[:, oc, :], TR[:, oc, :], pps[sl], ALU.add),
                        waits=[tmm, trunk_gate.get(oc)], inc=sVE)
                    pp_free[sl] = tev
                    res[oc] = tev
            return res

        bar5 = p.snap()
        res2 = ffn("f2", res1, bar=bar5)
        if DEBUG_DUMP == 2:
            dump_and_finish(res2)
            p.build(ctx)
            return nc

        xn3_state = {"t": None}

        def consume3(c, xn_ap, tn):
            ts = p.dma("scalar", xn3[c * 128:(c + 1) * 128, :], xn_ap,
                       waits=[tn])
            xn3_state["t"] = ts
            return ts

        bar6 = p.snap()
        ln_trunk("l3", res2, consume3, bar=bar6)
        bar6b = p.snap()
        qb_ready = q_production("qb", wqb, xn3, qb_sp, xn3_state["t"],
                                bar=bar6b)

        bar7 = p.snap()
        ctxb_ready = {}
        halves_b = [(slice(0, TOK), [(t, None) for t in range(64)])]
        from contextlib import ExitStack as _ES2
        with _ES2() as p7ctx:
            _e = p7ctx.enter_context
            BKT = _e(nc.sbuf_tensor("b_kt", [128, 64, 128], BF16))
            BVG = _e(nc.sbuf_tensor("b_vg", [128, 64, 128], BF16))
            BQB = _e(nc.sbuf_tensor("b_qb", [128, 2, TOK], BF16))
            BPB = _e(nc.sbuf_tensor("b_p", [128, 2, TOK], BF16))
            BDA = _e(nc.sbuf_tensor("b_da", [128, 1, TOK], F32))
            BDRS = _e(nc.sbuf_tensor("b_drs", [128, TOK], F32))
            BDNR = BDRS[0:1, :]
            BCX = _e(nc.sbuf_tensor("b_cx", [128, 2, TOK], BF16))
            bs0 = PS0[:, 0:TOK]
            bs1 = PS0[:, TOK:SEQ]
            bcps = PS1[:, 0:TOK]
            bdnp = PS1[:, TOK:SEQ]

            fsb = {"qbuf": BQB, "qfree": [bar7, bar7], "sc_ps": [bs0, bs1],
                   "sc_free": [bar7, bar7], "p_sb": BPB,
                   "p_free": [[bar7], [bar7]],
                   "pi": 0, "dacc": BDA, "dacc_free": [bar7], "dn_ps": bdnp,
                   "dn_row": BDNR, "dn_free": bar7, "dr_sb": BDRS,
                   "dr_sb_free": bar7, "ctx_ps": bcps, "ctx_sb": BCX,
                   "co_free": [bar7, bar7], "ctx_gate": bar7}

            kv_gate = bar7
            for g in range(NKV):
                tk = p.dma_emit("sync", lambda e, g=g: e.dma_start_transpose(
                    BKT[:].rearrange("p c w -> p (c w)"),
                    kvg[0, g].rearrange("a b -> (a b)").rearrange(
                        "(t d) -> t d", d=128)),
                    waits=[gst["done"], kv_gate])
                tv = p.dma("sync", BVG[:],
                           kvg[1, g].rearrange("(c b) (z d) -> (b z) c d",
                                               b=8, d=128),
                           waits=[gst["done"], kv_gate])
                kv_gate = flash(
                    fsb, [4 * g + i for i in range(4)],
                    lambda t: BKT[:, t, :], lambda t: BVG[:, t, :],
                    qb_sp, qb_ready, [tk, tv], [tk, tv], halves_b, None,
                    ctx_sp, ctxb_ready)

        bar8 = p.snap()
        res3 = wo_residual("wob", wob, ctxb_ready, res2, bar=bar8)
        if DEBUG_DUMP == 3:
            dump_and_finish(res3)
            p.build(ctx)
            return nc

        bar9 = p.snap()
        res4 = ffn("f4", res3, bar=bar9)

        dump_and_finish(res4)
        p.build(ctx)
    return nc


def _host_tables():
    inv = 1.0 / (10000.0 ** (np.arange(0, HD, 2, dtype=np.float64) / HD))
    pos = np.arange(SEQ, dtype=np.float64)
    ang = np.outer(inv, pos)
    emb = np.concatenate([ang, ang], axis=0)
    return np.cos(emb), np.sin(emb)


def _core_positions(core):
    b = core // 2
    if core % 2 == 0:
        own = np.r_[0:512, 1536:2048]
    else:
        own = np.r_[512:1536]
    oth = np.setdiff1d(np.arange(SEQ), own)
    return b, own, oth


def _tri_masks():
    m = np.zeros((4, 128, 512), dtype=np.float32)
    for j in range(4):
        m[j] = (j * 128 + np.arange(128)[:, None] <= np.arange(512)[None, :])
    return m.astype(ml_dtypes.bfloat16)


def _mscal_for(core):
    v = np.zeros(8, dtype=np.float32)
    v[0:4] = 0.0 if core % 2 == 0 else 1.0
    v[4:8] = 1.0 if core % 2 == 0 else 0.0
    return np.broadcast_to(v, (128, 8)).copy()


def _check_sched(core, ownpos, locpos):
    tri = _tri_masks().astype(np.float32)
    ms = _mscal_for(core)[0]
    for half in range(2):
        qpos = ownpos[half * 512:(half + 1) * 512]
        cover = np.zeros((SEQ, 512), dtype=np.float64)
        for s, tile in enumerate(SCHED_A[half]):
            slot = MSLOT_A[half][s]
            tloc = np.arange(tile * 128, (tile + 1) * 128)
            if slot is None:
                keep = np.ones((128, 512))
            elif slot[0] == "tri":
                keep = tri[slot[1]]
            else:
                keep = np.full((128, 512), ms[slot[1]])
            cover[tloc[:, None], np.arange(512)[None, :]] += keep
        want = (locpos[:, None] <= qpos[None, :]).astype(np.float64)
        assert (cover == want).all(), f"flash-a schedule mismatch core {core}"


_NC_CACHE = {}


def kernel(L0_attn_a_hidden_states, L0_attn_a_Wq, L0_attn_a_Wk, L0_attn_a_Wv,
           L0_attn_a_Wo, L0_attn_b_Wq, L0_attn_b_Wo, L0_attn_b_kv_cache,
           L0_attn_b_block_table, L0_ffn_A, L0_ffn_B, L0_ffn_Wr, L0_ffn_br):
    x = np.asarray(L0_attn_a_hidden_states, dtype=np.float32)
    f32 = lambda a: np.ascontiguousarray(np.asarray(a, dtype=np.float32))
    consts = {
        "wqaT": f32(np.asarray(L0_attn_a_Wq).T),
        "wkT": f32(np.asarray(L0_attn_a_Wk).T),
        "wvT": f32(np.asarray(L0_attn_a_Wv).T),
        "woaT": f32(np.asarray(L0_attn_a_Wo).T),
        "wqb": f32(L0_attn_b_Wq),
        "wob": f32(L0_attn_b_Wo),
        "aT": f32(np.asarray(L0_ffn_A).reshape(128, D).T),
        "bbig": f32(np.asarray(L0_ffn_B).transpose(0, 2, 1).reshape(128, D)),
        "wrT": f32(np.asarray(L0_ffn_Wr).T),
        "br": f32(np.asarray(L0_ffn_br).reshape(8, 1)),
        "kvc": f32(np.asarray(L0_attn_b_kv_cache)
                   .transpose(1, 2, 0, 3, 4).reshape(2, NKV, NBLK, BLKSZ * HD)),
        "identF": np.eye(128, dtype=np.float32),
        "identB": np.eye(128, dtype=ml_dtypes.bfloat16),
        "onesB": np.ones((128, 1), dtype=ml_dtypes.bfloat16),
        "onesRF": np.ones((1, 128), dtype=np.float32),
    }
    bt = np.asarray(L0_attn_b_block_table, dtype=np.int32)
    consts["btc"] = np.ascontiguousarray(bt.reshape(4, 128).T).view(np.uint32)
    consts["e8"] = (np.kron(np.eye(8), np.ones((1, 16))) * 16.0).astype(
        ml_dtypes.bfloat16)
    rotTm = np.zeros((128, 128), dtype=np.float32)
    for i in range(64):
        rotTm[64 + i, i] = -1.0
        rotTm[i, 64 + i] = 1.0
    consts["rotT"] = rotTm.astype(ml_dtypes.bfloat16)

    cosg, sing = _host_tables()
    in_maps = []
    metas = []
    for core in range(8):
        b, own, oth = _core_positions(core)
        loc = np.concatenate([own, oth])
        if core < 2:
            _check_sched(core, own, loc)
        m = dict(consts)
        m["xT"] = np.ascontiguousarray(x[b].T[:, loc])
        m["cosf"] = np.ascontiguousarray(cosg[:, loc]).astype(ml_dtypes.bfloat16)
        m["sinf"] = np.ascontiguousarray(sing[:, loc]).astype(ml_dtypes.bfloat16)
        m["coso"] = np.ascontiguousarray(cosg[:, own]).astype(ml_dtypes.bfloat16)
        m["sino"] = np.ascontiguousarray(sing[:, own]).astype(ml_dtypes.bfloat16)
        m["masks"] = _tri_masks()
        m["mscal"] = _mscal_for(core)
        in_maps.append(m)
        metas.append((b, own))

    if "nc" not in _NC_CACHE:
        _NC_CACHE["nc"] = build_program()
    nc = _NC_CACHE["nc"]
    res = run_bass_kernel_spmd(nc, in_maps, core_ids=list(range(8)))

    out = np.zeros((4, SEQ, D), dtype=np.float32)
    key = "dbg" if DEBUG_DUMP else "outT"
    for core in range(8):
        b, own = metas[core]
        out[b, own, :] = res.results[core][key].T
    return out


# version 05
# speedup vs baseline: 1.5004x; 1.5004x over previous
import math
import numpy as np
import ml_dtypes
import concourse.bass as bass
import concourse.mybir as mybir
from concourse.bass_core_rust import IndirectOffsetOnAxis
from concourse.bass_utils import run_bass_kernel_spmd

F32 = mybir.dt.float32
BF16 = mybir.dt.bfloat16
U32 = mybir.dt.uint32
AF = mybir.ActivationFunctionType
ALU = mybir.AluOpType
AX = mybir.AxisListType

D = 4096
NCHUNK = 32
TOK = 1024
SEQ = 2048
NH = 32
NKV = 8
HD = 128
NBLK = 1024
BLKSZ = 16
NPAGES = 512
INVSQ = 1.0 / math.sqrt(128.0)
CSTAB = 12.0
EPS = 1e-5

SCHED_A = [[8, 9, 10, 11, 0, 1, 2, 3], list(range(16))]
MSLOT_A = [
    [("scal", 0), ("scal", 1), ("scal", 2), ("scal", 3),
     ("tri", 0), ("tri", 1), ("tri", 2), ("tri", 3)],
    [None, None, None, None,
     ("tri", 0), ("tri", 1), ("tri", 2), ("tri", 3),
     None, None, None, None,
     ("scal", 4), ("scal", 5), ("scal", 6), ("scal", 7)],
]

import os
DEBUG_DUMP = int(os.environ.get("KDBG", "0"))
QPN = int(os.environ.get("KQPN", "16"))
KROPE = int(os.environ.get("KROPE", "0"))


class P:

    ENG = ("sync", "scalar", "vector", "tensor", "gpsimd")

    def __init__(self, nc):
        self.nc = nc
        self.ops = {e: [] for e in self.ENG}
        self.sems = {}
        self.cnt = {}
        self.lastw = {}

    def sem(self, name, ctx):
        self.sems[name] = ctx.enter_context(self.nc.semaphore(name))
        self.cnt[name] = 0
        return name

    def emit(self, eng, fn, waits=(), inc=None, n=1):
        wl = []
        flat = []
        for t in waits:
            if t is None:
                continue
            if isinstance(t, list):
                flat.extend(x for x in t if x is not None)
            else:
                flat.append(t)
        for t in flat:
            s, v = t
            k = (eng, s)
            if self.lastw.get(k, 0) < v:
                self.lastw[k] = v
                wl.append((s, v))
        tok = None
        if inc is not None:
            self.cnt[inc] += n
            tok = (inc, self.cnt[inc])
        self.ops[eng].append((wl, fn, inc, n))
        return tok

    def init_lanes(self, ctx, spec):
        self.lanes = {}
        self.lane_i = {}
        for eng, n in spec.items():
            self.lanes[eng] = [self.sem(f"dq_{eng}{i}", ctx) for i in range(n)]
            self.lane_i[eng] = 0

    def dma_emit(self, eng, fn, waits=()):
        lane = self.lanes[eng][self.lane_i[eng] % len(self.lanes[eng])]
        self.lane_i[eng] += 1
        base = self.cnt[lane]
        w = list(waits) + ([(lane, base)] if base else [])
        return self.emit(eng, fn, waits=w, inc=lane, n=16)

    def snap(self):
        return [(k, v) for (k, v) in self.cnt.items() if v > 0]

    def dma(self, eng, out, in_, waits=(), **kw):
        return self.dma_emit(
            eng, lambda e: e.dma_start(out=out, in_=in_, **kw), waits=waits)

    def build(self, ctx):
        block = ctx.enter_context(self.nc.Block())
        prog = self

        def runner(ename):
            def run(e):
                for (wl, fn, inc, n) in prog.ops[ename]:
                    for (s, v) in wl:
                        e.wait_ge(prog.sems[s], v)
                    ins = fn(e)
                    if inc is not None:
                        ins.then_inc(prog.sems[inc], n)
            return run

        block.sync(runner("sync"))
        block.scalar(runner("scalar"))
        block.vector(runner("vector"))
        block.tensor(runner("tensor"))
        block.gpsimd(runner("gpsimd"))


def cdiv(a, b):
    return (a + b - 1) // b


def build_program():
    nc = bass.Bass("TRN2", target_bir_lowering=False)
    dt = nc.dram_tensor
    xT = dt("xT", [D, SEQ], F32, kind="ExternalInput").ap()
    wqaT = dt("wqaT", [D, D], F32, kind="ExternalInput").ap()
    wkT = dt("wkT", [D, NKV * HD], F32, kind="ExternalInput").ap()
    wvT = dt("wvT", [D, NKV * HD], F32, kind="ExternalInput").ap()
    woaT = dt("woaT", [D, D], F32, kind="ExternalInput").ap()
    wqb = dt("wqb", [D, D], F32, kind="ExternalInput").ap()
    wob = dt("wob", [D, D], F32, kind="ExternalInput").ap()
    aT = dt("aT", [D, 128], F32, kind="ExternalInput").ap()
    bbig = dt("bbig", [128, D], F32, kind="ExternalInput").ap()
    wrT = dt("wrT", [D, 8], F32, kind="ExternalInput").ap()
    brD = dt("br", [8, 1], F32, kind="ExternalInput").ap()
    kvc = dt("kvc", [2, NKV, NBLK, BLKSZ * HD], F32, kind="ExternalInput").ap()
    btc = dt("btc", [128, 4], U32, kind="ExternalInput").ap()
    cosf = dt("cosf", [128, SEQ], BF16, kind="ExternalInput").ap()
    sinf = dt("sinf", [128, SEQ], BF16, kind="ExternalInput").ap()
    coso = dt("coso", [128, TOK], BF16, kind="ExternalInput").ap()
    sino = dt("sino", [128, TOK], BF16, kind="ExternalInput").ap()
    identF = dt("identF", [128, 128], F32, kind="ExternalInput").ap()
    identB = dt("identB", [128, 128], BF16, kind="ExternalInput").ap()
    rotT = dt("rotT", [128, 128], BF16, kind="ExternalInput").ap()
    e8 = dt("e8", [8, 128], BF16, kind="ExternalInput").ap()
    onesB = dt("onesB", [128, 1], BF16, kind="ExternalInput").ap()
    onesRF = dt("onesRF", [1, 128], F32, kind="ExternalInput").ap()
    onesRB = dt("onesRB", [1, 128], BF16, kind="ExternalInput").ap()
    masksD = dt("masks", [4, 128, 512], BF16, kind="ExternalInput").ap()
    mscalD = dt("mscal", [128, 8], F32, kind="ExternalInput").ap()
    outT = dt("outT", [D, TOK], F32, kind="ExternalOutput").ap()

    xn1 = dt("xn1", [D, SEQ], BF16, kind="Internal").ap()
    xn3 = dt("xn3", [D, TOK], BF16, kind="Internal").ap()
    kvg = dt("kvg", [2, NKV, NPAGES, BLKSZ * HD], BF16, kind="Internal").ap()
    qa_sp = dt("qa_sp", [NH, 128, TOK], BF16, kind="Internal").ap()
    qb_sp = dt("qb_sp", [NH, 128, TOK], BF16, kind="Internal").ap()
    ctx_sp = dt("ctx_sp", [NH, 128, TOK], BF16, kind="Internal").ap()

    dbg = dt("dbg", [D, TOK], F32, kind="ExternalOutput").ap() if DEBUG_DUMP else None

    from contextlib import ExitStack
    ctx = ExitStack()
    with ctx:
        E = ctx.enter_context
        p = P(nc)
        sSY = p.sem("sSY", ctx)
        sSC = p.sem("sSC", ctx)
        sGP = p.sem("sGP", ctx)
        sPE = p.sem("sPE", ctx)
        sAC = p.sem("sAC", ctx)
        sVE = p.sem("sVE", ctx)
        sPO = p.sem("sPO", ctx)
        p.init_lanes(ctx, {"sync": 8, "scalar": 6, "gpsimd": 6})

        PS0 = E(nc.psum_tensor("PS0", [128, SEQ], F32))
        PS1 = E(nc.psum_tensor("PS1", [128, SEQ], F32))

        TR = E(nc.sbuf_tensor("TR", [128, NCHUNK, TOK], F32))
        IDF = E(nc.sbuf_tensor("IDF", [128, 128], F32))
        IDB = E(nc.sbuf_tensor("IDB", [128, 128], BF16))
        ROT = E(nc.sbuf_tensor("ROT", [128, 128], BF16))
        E8S = E(nc.sbuf_tensor("E8S", [8, 128], BF16))
        ONE = E(nc.sbuf_tensor("ONE", [128, 1], BF16))
        ONEF = E(nc.sbuf_tensor("ONEF", [128, 1], F32))
        ONR = E(nc.sbuf_tensor("ONR", [1, 128], F32))
        ONRB = E(nc.sbuf_tensor("ONRB", [1, 128], BF16))
        BRS = E(nc.sbuf_tensor("BRS", [8, 1], F32))
        BTS = E(nc.sbuf_tensor("BTS", [128, 4], U32))

        tc_last = []
        for (dst, src) in ((IDF, identF), (IDB, identB), (ROT, rotT),
                           (E8S, e8), (ONE, onesB), (ONR, onesRF),
                           (ONRB, onesRB), (BRS, brD), (BTS, btc)):
            tc_last.append(p.dma("sync", dst[:], src))
        t_const = tc_last
        EPSC = E(nc.sbuf_tensor("EPSC", [128, 1], F32))
        CSTB = E(nc.sbuf_tensor("CSTB", [128, 1], F32))
        p.emit("vector", lambda e: e.memset(EPSC[:], EPS), inc=sVE)
        p.emit("vector", lambda e: e.memset(CSTB[:], -CSTAB), inc=sVE)
        t_onef = p.emit("vector", lambda e: e.memset(ONEF[:], 1.0), inc=sVE)

        from contextlib import ExitStack as _ESg
        gstg_ctx = _ESg()
        GSTG = gstg_ctx.enter_context(
            nc.sbuf_tensor("GSTG", [128, 1, BLKSZ * HD], BF16))
        gjobs = [(kv, h, c) for kv in range(2) for h in range(NKV)
                 for c in range(4)]
        gst = {"i": 0, "free": [None, None], "done": None}

        def emit_gathers(njobs):
            for _ in range(njobs):
                i = gst["i"]
                if i >= len(gjobs):
                    return
                kv, h, c = gjobs[i]
                slot = 0
                stg = GSTG[:, 0, :]

                def g_fn(e, kv=kv, h=h, c=c, stg=stg):
                    flat = kvc.rearrange("a b c d -> (a b c) d")
                    return e.indirect_dma_start(
                        stg, None, flat,
                        IndirectOffsetOnAxis(BTS[:, c:c + 1], 0),
                        element_offset=(kv * NKV + h) * NBLK * (BLKSZ * HD))
                tg = p.dma_emit("gpsimd", g_fn,
                                waits=[t_const, gst["free"][slot]])
                ts = p.dma("sync", kvg[kv, h, c * 128:(c + 1) * 128, :], stg,
                           waits=[tg])
                gst["free"][slot] = ts
                gst["done"] = ts
                gst["i"] = i + 1

        emit_gathers(8)

        def ln_rows(N, t_stats, psA, psB, rows, reps):
            sum_ps = psA[0:1, :]
            sq_ps = psB[0:1, :]
            reps_ps = (psA, psB)
            """rows: 3 SBUF [1, N] slots (m, msq, var/r in place)."""
            m_row, msq_row, var_row = rows
            r_row = var_row
            t1 = p.emit("scalar", lambda e: e.activation(
                m_row, sum_ps, AF.Copy, scale=1.0 / D), waits=[t_stats], inc=sAC)
            t2 = p.emit("scalar", lambda e: e.activation(
                msq_row, sq_ps, AF.Copy, scale=1.0 / D), waits=[t_stats, t1],
                inc=sAC)
            t3 = p.emit("vector", lambda e: e.tensor_tensor(
                var_row, m_row, m_row, ALU.mult), waits=[t1], inc=sVE)
            t4 = p.emit("vector", lambda e: e.tensor_tensor(
                var_row, msq_row, var_row, ALU.subtract), waits=[t2, t3],
                inc=sVE)
            t5 = p.emit("scalar", lambda e: e.activation(
                r_row, var_row, AF.Ln, bias=EPSC[0:1, :]),
                waits=[t4, t_onef], inc=sAC)
            t6 = p.emit("scalar", lambda e: e.activation(
                r_row, r_row, AF.Exp, scale=-0.5), waits=[t5], inc=sAC)
            tm = tr = None
            for j in range(cdiv(N, 512)):
                sl = slice(j * 512, (j + 1) * 512)
                tm = p.emit("tensor", lambda e, sl=sl: e.matmul(
                    reps_ps[0][:, sl], ONR[:], m_row[:, sl], start=True,
                    stop=True), waits=[t1, t2, t_onef], inc=sPE)
                tr = p.emit("tensor", lambda e, sl=sl: e.matmul(
                    reps_ps[1][:, sl], ONR[:], r_row[:, sl], start=True,
                    stop=True), waits=[t6], inc=sPE)
            t_mrep = p.emit("scalar", lambda e: e.copy(reps[0], reps_ps[0]),
                            waits=[tm, tr], inc=sAC)
            t_rrep = p.emit("scalar", lambda e: e.copy(reps[1], reps_ps[1]),
                            waits=[tr], inc=sAC)
            return t_mrep, t_rrep

        from contextlib import ExitStack as _ES
        with _ES() as p1ctx:
            _e = p1ctx.enter_context
            xc = _e(nc.sbuf_tensor("p1_xc", [128, 2, SEQ], F32))
            xbf = _e(nc.sbuf_tensor("p1_xb", [128, 2, SEQ], BF16))
            rows1 = _e(nc.sbuf_tensor("p1_rows", [1, 3, SEQ], F32))
            reps1 = _e(nc.sbuf_tensor("p1_reps", [128, 2, SEQ], F32))
            xnb = _e(nc.sbuf_tensor("p1_xn", [128, 2, SEQ], BF16))
            mrep_ps = PS0[:, :]
            rrep_ps = PS1[:, :]
            sum_ps = mrep_ps
            sq_ps = rrep_ps

            ld_gate = [None, None]
            pe_gate = [None, None]
            t_stats = None
            trunk_init = {}
            for c in range(NCHUNK):
                sl = c % 2
                tl = p.dma("sync", xc[:, sl, :], xT[c * 128:(c + 1) * 128, :],
                           waits=[ld_gate[sl]])
                trunk_init[c] = p.emit("vector", lambda e, c=c, sl=sl: e.tensor_copy(
                    TR[:, c, :], xc[:, sl, 0:TOK]), waits=[tl], inc=sVE)
                tb = p.emit("vector", lambda e, sl=sl: e.tensor_copy(
                    xbf[:, sl, :], xc[:, sl, :]), waits=[tl, pe_gate[sl]],
                    inc=sVE)
                tsum = None
                for j in range(4):
                    s5 = slice(j * 512, (j + 1) * 512)
                    tsum = p.emit("tensor", lambda e, sl=sl, s5=s5, c=c: e.matmul(
                        sum_ps[0:1, s5], ONE[:], xbf[:, sl, s5],
                        start=(c == 0), stop=(c == NCHUNK - 1)),
                        waits=[tb, t_const], inc=sPE)
                tq = p.emit("scalar", lambda e, sl=sl: e.activation(
                    xbf[:, sl, :], xbf[:, sl, :], AF.Square),
                    waits=[tsum], inc=sAC)
                for j in range(4):
                    s5 = slice(j * 512, (j + 1) * 512)
                    t_stats = p.emit("tensor", lambda e, sl=sl, s5=s5, c=c: e.matmul(
                        sq_ps[0:1, s5], ONE[:], xbf[:, sl, s5],
                        start=(c == 0), stop=(c == NCHUNK - 1)),
                        waits=[tq], inc=sPE)
                ld_gate[sl] = tb
                pe_gate[sl] = t_stats
                if c % 8 == 3:
                    emit_gathers(2)

            rows = [rows1[:, i, :] for i in range(3)]
            t_mrep, t_rrep = ln_rows(SEQ, t_stats, mrep_ps, rrep_ps,
                                     rows, (reps1[:, 0, :], reps1[:, 1, :]))

            ld2 = list(ld_gate)
            st2 = [None, None]
            xn1_done = None
            for c in range(NCHUNK):
                sl = c % 2
                tl = p.dma("scalar", xc[:, sl, :], xT[c * 128:(c + 1) * 128, :],
                           waits=[ld2[sl]])
                td = p.emit("vector", lambda e, sl=sl: e.tensor_tensor(
                    xc[:, sl, :], xc[:, sl, :], reps1[:, 0, :], ALU.subtract),
                    waits=[tl, t_mrep, t_rrep], inc=sVE)
                tn = p.emit("vector", lambda e, sl=sl: e.tensor_tensor(
                    xnb[:, sl, :], xc[:, sl, :], reps1[:, 1, :], ALU.mult),
                    waits=[td, st2[sl]], inc=sVE)
                ld2[sl] = tn
                ts = p.dma("sync", xn1[c * 128:(c + 1) * 128, :], xnb[:, sl, :],
                           waits=[tn])
                st2[sl] = ts
                xn1_done = ts
                if c % 8 == 3:
                    emit_gathers(2)

        def q_production(tag, wT_dram, xn_dram, q_spill, t_xn_ready,
                         gather_hook=None, bar=None):
            spills = {}
            bar = bar or []
            from contextlib import ExitStack as _ES
            with _ES() as qpctx:
                _e = qpctx.enter_context
                ws = _e(nc.sbuf_tensor(f"{tag}_ws", [128, 2, NCHUNK, 256], BF16))
                xcq = _e(nc.sbuf_tensor(f"{tag}_xc", [128, 8, TOK], BF16))
                qbr = _e(nc.sbuf_tensor(f"{tag}_qb", [128, TOK], BF16))
                t1s = _e(nc.sbuf_tensor(f"{tag}_t1", [128, TOK], F32))
                t2s = _e(nc.sbuf_tensor(f"{tag}_t2", [128, TOK], F32))
                qo = _e(nc.sbuf_tensor(f"{tag}_qo", [128, 2, TOK], BF16))
                cs = _e(nc.sbuf_tensor(f"{tag}_cs", [128, 2, TOK], BF16))
                tcs = p.dma("sync", cs[:, 0, :], coso, waits=[bar])
                tsn = p.dma("sync", cs[:, 1, :], sino, waits=[bar])
                qreg = [[PS0[:, 0:TOK], PS0[:, TOK:SEQ]],
                        [PS1[:, 0:TOK], PS1[:, TOK:SEQ]]]
                wfree = [bar, bar]
                xfree = [bar, bar]
                qofree = [bar, bar]
                st = {"last_rot": bar, "last_qbr": bar, "last_tv3": bar}
                region_free = {(-2, 0): bar, (-2, 1): bar, (-1, 0): bar,
                               (-1, 1): bar}
                mm_of = {}

                def emit_rope(ps):
                    qps = qreg[ps % 2]
                    for hh in range(2):
                        h = ps * 2 + hh
                        slq = hh
                        te = p.emit("scalar", lambda e, qps=qps, hh=hh: e.copy(
                            qbr[:], qps[hh]),
                            waits=[mm_of[(ps, hh)], st["last_rot"],
                                   st["last_qbr"]], inc=sAC)
                        trm = None
                        for j in range(2):
                            s5 = slice(j * 512, (j + 1) * 512)
                            trm = p.emit("tensor", lambda e, s5=s5, qps=qps, hh=hh: e.matmul(
                                qps[hh][:, s5], ROT[:], qbr[:, s5],
                                start=True, stop=True),
                                waits=[te, st["last_rot"]], inc=sPE)
                        st["last_qbr"] = trm
                        tv1 = p.emit("vector", lambda e: e.tensor_tensor(
                            t1s[:], qbr[:], cs[:, 0, :], ALU.mult),
                            waits=[te, tcs, st["last_tv3"]], inc=sVE)
                        tv2 = p.emit("vector", lambda e, qps=qps, hh=hh: e.tensor_tensor(
                            t2s[:], qps[hh], cs[:, 1, :], ALU.mult),
                            waits=[trm, tsn, tv1], inc=sVE)
                        tv3 = p.emit("vector", lambda e, slq=slq: e.tensor_tensor(
                            qo[:, slq, :], t1s[:], t2s[:], ALU.add),
                            waits=[tv2, qofree[slq]], inc=sVE)
                        st["last_tv3"] = tv3
                        st["last_rot"] = tv2
                        region_free[(ps, hh)] = tv2
                        tsp = p.dma("scalar", q_spill[h], qo[:, slq, :],
                                    waits=[tv3])
                        qofree[slq] = tsp
                        spills[h] = tsp

                for ps in range(QPN):
                    slw = ps % 2
                    qps = qreg[ps % 2]
                    if gather_hook:
                        gather_hook(3)
                    tw = p.dma("gpsimd", ws[:, slw, :, :],
                               wT_dram[:, ps * 256:(ps + 1) * 256].rearrange(
                                   "(c p) w -> p c w", p=128),
                               waits=[wfree[slw]])
                    mm_tok = [None, None]
                    for c4 in range(NCHUNK // 4):
                        hb = c4 % 2
                        tl = p.dma(
                            "sync", xcq[:, hb * 4:hb * 4 + 4, :],
                            xn_dram[c4 * 512:(c4 + 1) * 512, 0:TOK].rearrange(
                                "(c p) t -> p c t", p=128),
                            waits=[t_xn_ready, xfree[hb]])
                        for cs4 in range(4):
                            c = c4 * 4 + cs4
                            for hh in range(2):
                                wmm = [tl, tw, region_free[(ps - 2, hh)]]
                                for j in range(2):
                                    s5 = slice(j * 512, (j + 1) * 512)
                                    mm_tok[hh] = p.emit(
                                        "tensor",
                                        lambda e, hh=hh, c=c, s5=s5, cs4=cs4, hb=hb, slw=slw, qps=qps: e.matmul(
                                            qps[hh][:, s5],
                                            ws[:, slw, c, hh * 128:(hh + 1) * 128],
                                            xcq[:, hb * 4 + cs4, s5],
                                            start=(c == 0), stop=(c == NCHUNK - 1)),
                                        waits=wmm, inc=sPE)
                        xfree[hb] = mm_tok[1]
                    wfree[slw] = mm_tok[1]
                    mm_of[(ps, 0)] = mm_tok[0]
                    mm_of[(ps, 1)] = mm_tok[1]
                    if ps > 0:
                        emit_rope(ps - 1)
                emit_rope(QPN - 1)
            return spills

        def flash(s, qheads, kt_fn, v_fn, q_spill, q_ready, t_k, t_v,
                  halves, msk, ctx_dst, ctx_ready, t_msk=None):
            LOOK = 2
            for qh in qheads:
                tql = p.dma("sync", s["qbuf"][:, qh % 2, :], q_spill[qh],
                            waits=[q_ready[qh], s["qfree"][qh % 2]])
                ctx_tok = None
                pend = []
                prev_d = {}
                dacc_of = {}

                def consume():
                    nonlocal ctx_tok
                    (half, si, nst, tile, slot, pp, scb, pb, cols, tmm) = \
                        pend.pop(0)
                    (half2, si2, nst2, tile2, slot2, pp2, scb2, pb2, cols2,
                     tmm2) = pend.pop(0)
                    assert pp2 == pp + 1 and pp % 2 == 0
                    tex = p.emit("scalar",
                                 lambda e, pp=pp: e.activation(
                                     s["p_pair"][pp // 2],
                                     s["sc_pair"][pp // 2], AF.Exp,
                                     bias=CSTB[:], scale=INVSQ),
                                 waits=[tmm, tmm2, t_onef] + s["p_free"][pp]
                                 + s["p_free"][pp2], inc=sAC)
                    s["sc_free"][pp] = tex
                    s["sc_free"][pp2] = tex
                    for (half, si, nst, tile, slot, pp, scb, pb, cols, tmm) \
                            in ((half, si, nst, tile, slot, pp, scb, pb, cols,
                                 tmm),
                                (half2, si2, nst2, tile2, slot2, pp2, scb2,
                                 pb2, cols2, tmm2)):
                        tcons = tex
                        if slot is not None:
                            kind, j = slot
                            if kind == "tri":
                                tcons = p.emit(
                                    "vector",
                                    lambda e, pb=pb, j=j: e.tensor_tensor(
                                        pb, pb, msk[0][:, j, :], ALU.mult),
                                    waits=[tex, t_msk], inc=sVE)
                            else:
                                tcons = p.emit(
                                    "vector",
                                    lambda e, pb=pb, j=j: e.tensor_scalar(
                                        pb, pb, msk[1][:, j:j + 1], None,
                                        ALU.mult),
                                    waits=[tex, t_msk], inc=sVE)
                        dacc = dacc_of[half]
                        td = p.emit("vector", lambda e, dacc=dacc, pb=pb: e.tensor_tensor(
                            dacc, dacc, pb, ALU.add),
                            waits=[tcons, prev_d[half]], inc=sVE)
                        prev_d[half] = td
                        csl = cols
                        tav = p.emit(
                            "tensor",
                            lambda e, tile=tile, csl=csl, pb=pb, si=si, nst=nst: e.matmul(
                                s["ctx_ps"][:, csl], v_fn(tile), pb,
                                start=(si == 0), stop=(si == nst - 1)),
                            waits=[tcons, t_v, s["ctx_gate"]], inc=sPE)
                        s["p_free"][pp] = [tav, td]
                        ctx_tok = tav

                for half, (cols, steps) in enumerate(halves):
                    assert cols.stop - cols.start == 512
                    dacc = s["dacc"][:, half, :]
                    dacc_of[half] = dacc
                    tz = p.emit("vector", lambda e, dacc=dacc: e.memset(
                        dacc, 0.0), waits=[s["dacc_free"][half]], inc=sVE)
                    prev_d[half] = tz
                    for si, (tile, slot) in enumerate(steps):
                        pp = s["pi"] % 4
                        s["pi"] += 1
                        scb = s["sc_ps"][pp]
                        tmm = p.emit(
                            "tensor",
                            lambda e, tile=tile, cols=cols, scb=scb, qh=qh: e.matmul(
                                scb, kt_fn(tile),
                                s["qbuf"][:, qh % 2, cols],
                                start=True, stop=True),
                            waits=[tql, t_k, s["sc_free"][pp]], inc=sPE)
                        pend.append((half, si, len(steps), tile, slot, pp,
                                     scb, s["p_sb"][:, pp, :], cols, tmm))
                        if len(pend) >= 4:
                            consume()
                while pend:
                    consume()
                denom_mm = []
                for half, (cols, steps) in enumerate(halves):
                    dacc = dacc_of[half]
                    dhi = s["dhl"][:, 0, :]
                    dlo = s["dhl"][:, 1, :]
                    th1 = p.emit("vector", lambda e, dacc=dacc, dhi=dhi: e.tensor_copy(
                        dhi, dacc), waits=[prev_d[half], s["dhl_free"]],
                        inc=sVE)
                    th2 = p.emit("vector", lambda e, dacc=dacc, dhi=dhi, dlo=dlo: e.tensor_tensor(
                        dlo, dacc, dhi, ALU.subtract), waits=[th1], inc=sVE)
                    td1 = p.emit("tensor", lambda e, cols=cols, dhi=dhi: e.matmul(
                        s["dn_ps"][0:1, cols], ONE[:], dhi,
                        start=True, stop=False),
                        waits=[th1, t_const, s["dn_free"]], inc=sPE)
                    td2 = p.emit("tensor", lambda e, cols=cols, dlo=dlo: e.matmul(
                        s["dn_ps"][0:1, cols], ONE[:], dlo,
                        start=False, stop=True), waits=[th2], inc=sPE)
                    denom_mm.append(td2)
                    s["dacc_free"][half] = th2
                    s["dhl_free"] = td2
                rrow = s["rr"]
                trr = None
                last_split = s["rr_free"]
                for j in range(2):
                    s5 = slice(j * 512, (j + 1) * 512)
                    trc = p.emit("vector", lambda e, s5=s5: e.reciprocal(
                        rrow[0][0:1, 0:512], s["dn_ps"][0:1, s5]),
                        waits=denom_mm + [last_split, s["dr_sb_free"]],
                        inc=sVE)
                    th3 = p.emit("vector", lambda e: e.tensor_copy(
                        rrow[1][0:1, :], rrow[0][0:1, 0:512]), waits=[trc],
                        inc=sVE)
                    th4 = p.emit("vector", lambda e: e.tensor_tensor(
                        rrow[2][0:1, :], rrow[0][0:1, 0:512], rrow[1][0:1, :],
                        ALU.subtract), waits=[th3], inc=sVE)
                    p.emit("tensor", lambda e, s5=s5: e.matmul(
                        s["dn_ps"][:, s5], ONRB[:], rrow[1][0:1, :],
                        start=True, stop=False),
                        waits=[th3, th4, t_const], inc=sPE)
                    trr = p.emit("tensor", lambda e, s5=s5: e.matmul(
                        s["dn_ps"][:, s5], ONRB[:], rrow[2][0:1, :],
                        start=False, stop=True), waits=[th4], inc=sPE)
                    last_split = trr
                tre = p.emit("scalar", lambda e: e.copy(
                    s["dr_sb"][:], s["dn_ps"]),
                    waits=[trr, s["dr_sb_free"]], inc=sAC)
                s["dn_free"] = tre
                s["rr_free"] = trr
                tnm = p.emit("vector", lambda e, qh=qh: e.tensor_tensor(
                    s["ctx_sb"][:, qh % 2, :], s["ctx_ps"], s["dr_sb"][:],
                    ALU.mult), waits=[tre, ctx_tok, s["co_free"][qh % 2]],
                    inc=sVE)
                s["dr_sb_free"] = tnm
                s["ctx_gate"] = tnm
                tsp = p.dma("scalar", ctx_dst[qh], s["ctx_sb"][:, qh % 2, :],
                            waits=[tnm])
                s["co_free"][qh % 2] = tsp
                s["qfree"][qh % 2] = tsp
                ctx_ready[qh] = tsp
            return s["ctx_gate"]

        def wo_residual(tag, wo_dram, ctx_ready, trunk_gate, bar=None):
            res_tok = {}
            bar = bar or []
            from contextlib import ExitStack as _ES
            with _ES() as woctx:
                _e = woctx.enter_context
                wos = _e(nc.sbuf_tensor(f"{tag}_wo", [128, 2, NCHUNK, 256], BF16))
                cxs = _e(nc.sbuf_tensor(f"{tag}_cx", [128, 4, TOK], BF16))
                yps = [PS0[:, 0:TOK], PS0[:, TOK:SEQ], PS1[:, 0:TOK],
                       PS1[:, TOK:SEQ]]
                st_wo = {"psum_gate": [bar, bar],
                         "cfree": [bar, bar, bar, bar]}

                def _wo_grp(grp, tw):
                    slp = grp % 2
                    ypair = (yps[0], yps[1]) if slp == 0 else (yps[2], yps[3])
                    tmm = None
                    for h in range(NH):
                        slc = h % 4
                        tc = p.dma("sync", cxs[:, slc, :], ctx_sp[h],
                                   waits=[ctx_ready.get(h),
                                          st_wo["cfree"][slc]])
                        for j in range(2):
                            for q5 in range(2):
                                s5 = slice(q5 * 512, (q5 + 1) * 512)
                                tmm = p.emit(
                                    "tensor",
                                    lambda e, h=h, j=j, s5=s5, slc=slc, ypair=ypair: e.matmul(
                                        ypair[j][:, s5],
                                        wos[:, grp % 2, h, j * 128:(j + 1) * 128],
                                        cxs[:, slc, s5],
                                        start=(h == 0), stop=(h == NH - 1)),
                                    waits=[tw, tc, st_wo["psum_gate"][slp]],
                                    inc=sPE)
                        st_wo["cfree"][slc] = tmm
                    tev = None
                    for j in range(2):
                        oc = grp * 2 + j
                        tev = p.emit("vector", lambda e, j=j, oc=oc, ypair=ypair: e.tensor_tensor(
                            TR[:, oc, :], TR[:, oc, :], ypair[j], ALU.add),
                            waits=[tmm, trunk_gate.get(oc)], inc=sVE)
                        res_tok[oc] = tev
                    st_wo["psum_gate"][slp] = tev
                    return tev
                psum_gate = bar
                cfree = [bar, bar]
                wfree = [bar, bar]
                wtoks = {}
                for grp in range(16):
                    slw = grp % 2
                    if grp >= 2:
                        wfree[slw] = _wo_grp(grp - 2, wtoks[grp - 2])
                    wtoks[grp] = p.dma(
                        "gpsimd", wos[:, slw, :, :],
                        wo_dram[:, grp * 256:(grp + 1) * 256].rearrange(
                            "(c p) w -> p c w", p=128),
                        waits=[wfree[slw]])
                for gg in (14, 15):
                    _wo_grp(gg, wtoks[gg])
            return res_tok

        def ln_trunk(tag, trunk_gate, consume, pool_engine="vector",
                     bar=None):
            bar = bar or []
            from contextlib import ExitStack as _ES
            with _ES() as lnctx:
                _e = lnctx.enter_context
                xb2 = _e(nc.sbuf_tensor(f"{tag}_b", [128, 2, TOK], BF16))
                xq2 = _e(nc.sbuf_tensor(f"{tag}_q", [128, 2, TOK], BF16))
                rows2 = _e(nc.sbuf_tensor(f"{tag}_rows", [1, 3, TOK], F32))
                reps2 = _e(nc.sbuf_tensor(f"{tag}_reps", [128, 2, TOK], F32))
                xn2 = _e(nc.sbuf_tensor(f"{tag}_xn", [128, 2, TOK], BF16))
                dd2 = _e(nc.sbuf_tensor(f"{tag}_d", [128, 2, TOK], F32))
                mr2 = PS0[:, TOK:SEQ]
                rr2 = PS1[:, TOK:SEQ]
                pe_free = [bar, bar]
                t_stats = None
                for c in range(NCHUNK):
                    sl = c % 2
                    tb = p.emit(pool_engine, lambda e, c=c, sl=sl: e.tensor_copy(
                        xb2[:, sl, :], TR[:, c, :]),
                        waits=[trunk_gate.get(c), pe_free[sl]], inc=sPO)
                    tq = p.emit(pool_engine, lambda e, sl=sl: e.tensor_tensor(
                        xq2[:, sl, :], xb2[:, sl, :], xb2[:, sl, :], ALU.mult),
                        waits=[tb], inc=sPO)
                    for j in range(2):
                        s5 = slice(j * 512, (j + 1) * 512)
                        p.emit("tensor", lambda e, sl=sl, s5=s5, c=c: e.matmul(
                            mr2[0:1, s5], ONE[:], xb2[:, sl, s5],
                            start=(c == 0), stop=(c == NCHUNK - 1)),
                            waits=[tb, t_const], inc=sPE)
                        t_stats = p.emit("tensor", lambda e, sl=sl, s5=s5, c=c: e.matmul(
                            rr2[0:1, s5], ONE[:], xq2[:, sl, s5],
                            start=(c == 0), stop=(c == NCHUNK - 1)),
                            waits=[tq], inc=sPE)
                    pe_free[sl] = t_stats
                rows = [rows2[:, i, :] for i in range(3)]
                t_mrep, t_rrep = ln_rows(TOK, t_stats, mr2, rr2, rows,
                                         (reps2[:, 0, :], reps2[:, 1, :]))
                cons_tok = [bar, bar]
                for c in range(NCHUNK):
                    sl = c % 2
                    td = p.emit("vector", lambda e, c=c, sl=sl: e.tensor_tensor(
                        dd2[:, sl, :], TR[:, c, :], reps2[:, 0, :],
                        ALU.subtract),
                        waits=[trunk_gate.get(c), t_mrep, t_rrep, cons_tok[sl]],
                        inc=sVE)
                    tn = p.emit("vector", lambda e, sl=sl: e.tensor_tensor(
                        xn2[:, sl, :], dd2[:, sl, :], reps2[:, 1, :], ALU.mult),
                        waits=[td], inc=sVE)
                    cons_tok[sl] = consume(c, xn2[:, sl, :], tn)

        def dump_and_finish(gate):
            last = None
            tgt = dbg if DEBUG_DUMP else outT
            for c in range(NCHUNK):
                last = p.dma("sync", tgt[c * 128:(c + 1) * 128, :], TR[:, c, :],
                             waits=[gate.get(c)])
            tail = [last]
            for eng, lns in p.lanes.items():
                for ln in lns:
                    if p.cnt[ln]:
                        tail.append((ln, p.cnt[ln]))
            p.emit("sync", lambda e: e.nop(), waits=tail)

        if DEBUG_DUMP == 11:
            emit_gathers(len(gjobs))
            gstg_ctx.close()
            p.emit("vector", lambda e: e.engine_nop(),
                   waits=[xn1_done, gst["done"]], inc=sVE)
            dump_and_finish(trunk_init)
            p.build(ctx)
            return nc

        bar_p1 = p.snap()
        qa_ready = q_production("qa", wqaT, xn1, qa_sp, xn1_done,
                                gather_hook=emit_gathers, bar=bar_p1)
        emit_gathers(len(gjobs))
        gstg_ctx.close()
        if DEBUG_DUMP in (12, 121, 122):
            p.emit("vector", lambda e: e.engine_nop(),
                   waits=[qa_ready[QPN * 2 - 1], gst["done"]], inc=sVE)
            dump_and_finish(trunk_init)
            p.build(ctx)
            return nc

        bar3 = p.snap()
        ctxa_ready = {}
        halves_a = []
        for half in range(2):
            cols = slice(half * 512, half * 512 + 512)
            steps = [(SCHED_A[half][s], MSLOT_A[half][s])
                     for s in range(len(SCHED_A[half]))]
            halves_a.append((cols, steps))

        from contextlib import ExitStack as _ES
        with _ES() as p3ctx:
            _e = p3ctx.enter_context
            MSK = _e(nc.sbuf_tensor("a_msk", [128, 4, 512], BF16))
            MS = _e(nc.sbuf_tensor("a_ms", [128, 8], F32))
            CSF = _e(nc.sbuf_tensor("a_cs", [128, 2, SEQ], BF16))
            WKV = _e(nc.sbuf_tensor("a_wk", [128, NCHUNK, 256], BF16))
            XF = _e(nc.sbuf_tensor("a_xf", [128, 2, SEQ], BF16))
            KBR = _e(nc.sbuf_tensor("a_kb", [128, SEQ], BF16))
            VTD = KBR
            AT1 = _e(nc.sbuf_tensor("a_t1", [128, SEQ], BF16))
            KT = _e(nc.sbuf_tensor("a_kt", [128, SEQ], BF16))
            VG = _e(nc.sbuf_tensor("a_vg", [128, 16, 128], BF16))
            AQB = _e(nc.sbuf_tensor("a_qb", [128, 2, TOK], BF16))
            APB = _e(nc.sbuf_tensor("a_p", [128, 4, 512], BF16))
            ADA = _e(nc.sbuf_tensor("a_da", [128, 2, 512], F32))
            DHL = _e(nc.sbuf_tensor("a_dhl", [128, 2, 512], BF16))
            RHI = _e(nc.sbuf_tensor("a_rhi", [1, 512], BF16))
            RLO = _e(nc.sbuf_tensor("a_rlo", [1, 512], BF16))
            ADRS = _e(nc.sbuf_tensor("a_drs", [128, TOK], F32))
            ACX = _e(nc.sbuf_tensor("a_cx", [128, 2, TOK], BF16))

            p.dma("sync", MSK[:], masksD.rearrange("m p q -> p m q"),
                  waits=[gst["done"], bar3])
            tmsk = p.dma("sync", MS[:], mscalD, waits=[bar3])
            tcsf = p.dma("sync", CSF[:, 0, :], cosf, waits=[bar3])
            tsnf = p.dma("sync", CSF[:, 1, :], sinf, waits=[bar3])

            fs = {"qbuf": AQB, "qfree": [bar3, bar3], "sc_ps": None,
                  "sc_free": [bar3, bar3, bar3, bar3], "p_sb": APB,
                  "p_free": [[bar3], [bar3], [bar3], [bar3]], "pi": 0,
                  "dacc": ADA, "dacc_free": [bar3, bar3], "dn_ps": None,
                  "dn_free": bar3, "dr_sb": ADRS, "dr_sb_free": bar3,
                  "dhl": DHL, "dhl_free": bar3,
                  "rr": [ADRS[0:1, :], RHI, RLO], "rr_free": bar3,
                  "ctx_ps": None, "ctx_sb": ACX, "co_free": [bar3, bar3],
                  "ctx_gate": bar3}

            xf_free = [bar3, bar3]
            wkv_free = bar3
            kv_sbuf_gate = bar3
            APS = PS0[:, :]
            BPS = PS1[:, :]
            for g in range(NKV):
                twk = p.dma("gpsimd", WKV[:, :, 0:128],
                            wkT[:, g * 128:(g + 1) * 128].rearrange(
                                "(c p) w -> p c w", p=128),
                            waits=[wkv_free])
                twv = p.dma("gpsimd", WKV[:, :, 128:256],
                            wvT[:, g * 128:(g + 1) * 128].rearrange(
                                "(c p) w -> p c w", p=128),
                            waits=[wkv_free])
                tmmk = tmmv = None
                for c in range(NCHUNK):
                    sl = c % 2
                    tl = p.dma("sync" if c % 2 == 0 else "scalar",
                               XF[:, sl, :],
                               xn1[c * 128:(c + 1) * 128, :],
                               waits=[xn1_done, xf_free[sl]])
                    for j in range(4):
                        s5 = slice(j * 512, (j + 1) * 512)
                        tmmk = p.emit("tensor", lambda e, c=c, sl=sl, s5=s5: e.matmul(
                            APS[:, s5], WKV[:, c, 0:128], XF[:, sl, s5],
                            start=(c == 0), stop=(c == NCHUNK - 1)),
                            waits=[tl, twk, twv, kv_sbuf_gate], inc=sPE)
                        tmmv = p.emit("tensor", lambda e, c=c, sl=sl, s5=s5: e.matmul(
                            BPS[:, s5], WKV[:, c, 128:256], XF[:, sl, s5],
                            start=(c == 0), stop=(c == NCHUNK - 1)),
                            waits=[tl], inc=sPE)
                    xf_free[sl] = tmmv
                wkv_free = tmmv
                tke = p.emit("scalar", lambda e: e.copy(KBR[:], APS[:]),
                             waits=[tmmk, kv_sbuf_gate], inc=sAC)
                tv1 = p.emit("vector", lambda e: e.tensor_tensor(
                    AT1[:], KBR[:], CSF[:, 0, :], ALU.mult),
                    waits=[tke, tcsf, kv_sbuf_gate], inc=sVE)
                trm = None
                for j in range(4):
                    s5 = slice(j * 512, (j + 1) * 512)
                    trm = p.emit("tensor", lambda e, s5=s5: e.matmul(
                        APS[:, s5], ROT[:], KBR[:, s5], start=True,
                        stop=True), waits=[tke, tv1], inc=sPE)
                tve = p.emit("scalar", lambda e: e.copy(VTD[:], BPS[:]),
                             waits=[tmmv, trm, kv_sbuf_gate], inc=sAC)
                tv2 = p.emit("vector", lambda e: e.tensor_tensor(
                    KT[:], APS[:], CSF[:, 1, :], ALU.mult),
                    waits=[trm, tsnf, kv_sbuf_gate], inc=sVE)
                tkt = p.emit("vector", lambda e: e.tensor_tensor(
                    KT[:], KT[:], AT1[:], ALU.add),
                    waits=[tv2, tv1], inc=sVE)
                tvg = []
                for j in range(16):
                    tvg.append(p.dma_emit(
                        "sync",
                        lambda e, j=j: e.dma_start_transpose(
                            VG[:, j, :], VTD[:, j * 128:(j + 1) * 128]),
                        waits=[tve, kv_sbuf_gate]))
                fs["sc_ps"] = [APS[:, 0:512], APS[:, 512:1024],
                               BPS[:, 1024:1536], BPS[:, 1536:2048]]
                fs["sc_pair"] = [APS[:, 0:1024], BPS[:, 1024:2048]]
                fs["p_pair"] = [
                    APB[:, 0:2, :].rearrange("p a b -> p (a b)"),
                    APB[:, 2:4, :].rearrange("p a b -> p (a b)")]
                fs["ctx_ps"] = BPS[:, 0:TOK]
                fs["dn_ps"] = APS[:, TOK:SEQ]
                kv_sbuf_gate = flash(
                    fs, [4 * g + i for i in range(4)],
                    lambda t: KT[:, t * 128:(t + 1) * 128],
                    lambda t: VG[:, t, :], qa_sp, qa_ready,
                    [tkt, tke, tv2, tve], tvg, halves_a, (MSK, MS), ctx_sp,
                    ctxa_ready, t_msk=tmsk)

        if DEBUG_DUMP == 13:
            p.emit("vector", lambda e: e.engine_nop(),
                   waits=[ctxa_ready[NH - 1]], inc=sVE)
            dump_and_finish(trunk_init)
            p.build(ctx)
            return nc
        bar4 = p.snap()
        res1 = wo_residual("woa", woaT, ctxa_ready, {}, bar=bar4)

        if DEBUG_DUMP == 1:
            dump_and_finish(res1)
            p.build(ctx)
            return nc


        def ffn(tag, trunk_gate, bar=None):
            res = {}
            bar = bar or []
            from contextlib import ExitStack as _ES
            with _ES() as ffctx:
                _e = ffctx.enter_context
                ATS = _e(nc.sbuf_tensor(f"{tag}_at", [128, NCHUNK, 128], BF16))
                BBS = _e(nc.sbuf_tensor(f"{tag}_bb", [128, NCHUNK, 128], BF16))
                WRS = _e(nc.sbuf_tensor(f"{tag}_wr", [128, NCHUNK, 8], BF16))
                t_at = p.dma("gpsimd", ATS[:],
                             aT.rearrange("(c p) w -> p c w", p=128),
                             waits=[bar])
                t_bb = p.dma("gpsimd", BBS[:],
                             bbig.rearrange("p (c w) -> p c w", w=128),
                             waits=[bar])
                t_wr = p.dma("gpsimd", WRS[:],
                             wrT.rearrange("(c p) w -> p c w", p=128),
                             waits=[bar])
                RR = _e(nc.sbuf_tensor(f"{tag}_rr", [8, TOK], F32))
                RTOK = _e(nc.sbuf_tensor(f"{tag}_rt", [128, 8, 8], F32))
                MX = _e(nc.sbuf_tensor(f"{tag}_mx", [128, 8, 1], F32))
                EXT = _e(nc.sbuf_tensor(f"{tag}_ex", [128, 8, 8], F32))
                SM = _e(nc.sbuf_tensor(f"{tag}_sm", [128, 8, 1], F32))
                STOK = _e(nc.sbuf_tensor(f"{tag}_sb", [128, 8, 8], F32))
                STT = _e(nc.sbuf_tensor(f"{tag}_st", [8, TOK], BF16))
                SREP = _e(nc.sbuf_tensor(f"{tag}_sr", [128, TOK], F32))
                LOWB = _e(nc.sbuf_tensor(f"{tag}_lo", [128, TOK], BF16))
                psA = PS0[:, 0:TOK]
                psC = PS1[:, 0:TOK]

                fin_tok = []

                def consume(c, xn_ap, tn):
                    last = None
                    for j in range(2):
                        s5 = slice(j * 512, (j + 1) * 512)
                        p.emit("tensor", lambda e, c=c, s5=s5, xn_ap=xn_ap: e.matmul(
                            psC[0:8, s5], WRS[:, c, :], xn_ap[:, s5],
                            start=(c == 0), stop=(c == NCHUNK - 1)),
                            waits=[tn, t_wr, bar], inc=sPE)
                        last = p.emit("tensor", lambda e, c=c, s5=s5, xn_ap=xn_ap: e.matmul(
                            psA[:, s5], ATS[:, c, :], xn_ap[:, s5],
                            start=(c == 0), stop=(c == NCHUNK - 1)),
                            waits=[t_at], inc=sPE)
                    if c == NCHUNK - 1:
                        fin_tok.append(last)
                    return last

                ln_trunk(tag + "l", trunk_gate, consume, bar=bar)
                psB = PS0[:, TOK:SEQ]
                psD = PS1[:, TOK:SEQ]
                t_rl = fin_tok[0]
                t_rr = p.emit("vector", lambda e: e.tensor_scalar(
                    RR[:], psC[0:8, :], BRS[:], None, ALU.add),
                    waits=[t_rl, t_const], inc=sVE)
                tp_prev = None
                tp_toks = []
                for j in range(8):
                    tt = p.emit("tensor", lambda e, j=j: e.transpose(
                        psB[:, j * 8:(j + 1) * 8],
                        RR[:, j * 128:(j + 1) * 128], IDF[0:8, 0:8]),
                        waits=[t_rr, t_const, tp_prev], inc=sPE)
                    te = p.emit("scalar", lambda e, j=j: e.copy(
                        RTOK[:, j, :], psB[:, j * 8:(j + 1) * 8]), waits=[tt],
                        inc=sAC)
                    tp_prev = te
                    tp_toks.append(te)
                sm_toks = []
                for j in range(8):
                    tmx = p.emit("vector", lambda e, j=j: e.tensor_reduce(
                        MX[:, j, :], RTOK[:, j, :], AX.X, ALU.max, negate=True),
                        waits=[tp_toks[j]], inc=sVE)
                    tex = p.emit("scalar", lambda e, j=j: e.activation(
                        EXT[:, j, :], RTOK[:, j, :], AF.Exp, bias=MX[:, j, :]),
                        waits=[tmx], inc=sAC)
                    tsm = p.emit("vector", lambda e, j=j: e.tensor_reduce(
                        SM[:, j, :], EXT[:, j, :], AX.X, ALU.add),
                        waits=[tex], inc=sVE)
                    trc = p.emit("vector", lambda e, j=j: e.reciprocal(
                        SM[:, j, :], SM[:, j, :]), waits=[tsm], inc=sVE)
                    tmu = p.emit("vector", lambda e, j=j: e.tensor_scalar(
                        STOK[:, j, :], EXT[:, j, :], SM[:, j, :], None,
                        ALU.mult), waits=[trc], inc=sVE)
                    sm_toks.append(tmu)
                tb_prev = tp_prev
                tst = None
                for j in range(8):
                    tt = p.emit("tensor", lambda e, j=j: e.transpose(
                        psB[0:8, 512:640], STOK[:, j, :], IDF[:]),
                        waits=[sm_toks[j], tb_prev], inc=sPE)
                    tst = p.emit("scalar", lambda e, j=j: e.copy(
                        STT[:, j * 128:(j + 1) * 128], psB[0:8, 512:640]),
                        waits=[tt], inc=sAC)
                    tb_prev = tst
                tsr = None
                for j in range(2):
                    s5 = slice(j * 512, (j + 1) * 512)
                    tsr = p.emit("tensor", lambda e, s5=s5: e.matmul(
                        psB[:, s5], E8S[:], STT[:, s5], start=True, stop=True),
                        waits=[tst, t_const], inc=sPE)
                tse = p.emit("scalar", lambda e: e.copy(SREP[:], psB),
                             waits=[tsr], inc=sAC)
                tlo = p.emit("vector", lambda e: e.tensor_tensor(
                    LOWB[:], psA, SREP[:], ALU.mult),
                    waits=[fin_tok[0], tse], inc=sVE)
                pps = [psC, psD]
                pp_free = [t_rr, None]
                for oc in range(NCHUNK):
                    sl = oc % 2
                    tmm = None
                    for j in range(2):
                        s5 = slice(j * 512, (j + 1) * 512)
                        tmm = p.emit("tensor", lambda e, oc=oc, s5=s5, sl=sl: e.matmul(
                            pps[sl][:, s5], BBS[:, oc, :], LOWB[:, s5],
                            start=True, stop=True),
                            waits=[tlo, t_bb, pp_free[sl]], inc=sPE)
                    tev = p.emit("vector", lambda e, oc=oc, sl=sl: e.tensor_tensor(
                        TR[:, oc, :], TR[:, oc, :], pps[sl], ALU.add),
                        waits=[tmm, trunk_gate.get(oc)], inc=sVE)
                    pp_free[sl] = tev
                    res[oc] = tev
            return res

        bar5 = p.snap()
        res2 = ffn("f2", res1, bar=bar5)
        if DEBUG_DUMP == 2:
            dump_and_finish(res2)
            p.build(ctx)
            return nc

        xn3_state = {"t": None}

        def consume3(c, xn_ap, tn):
            ts = p.dma("scalar", xn3[c * 128:(c + 1) * 128, :], xn_ap,
                       waits=[tn])
            xn3_state["t"] = ts
            return ts

        bar6 = p.snap()
        ln_trunk("l3", res2, consume3, bar=bar6)
        bar6b = p.snap()
        qb_ready = q_production("qb", wqb, xn3, qb_sp, xn3_state["t"],
                                bar=bar6b)

        bar7 = p.snap()
        ctxb_ready = {}
        halves_b = [(slice(0, 512), [(t, None) for t in range(64)]),
                    (slice(512, TOK), [(t, None) for t in range(64)])]
        from contextlib import ExitStack as _ES2
        with _ES2() as p7ctx:
            _e = p7ctx.enter_context
            BKT = _e(nc.sbuf_tensor("b_kt", [128, 64, 128], BF16))
            BVG = _e(nc.sbuf_tensor("b_vg", [128, 64, 128], BF16))
            BQB = _e(nc.sbuf_tensor("b_qb", [128, 2, TOK], BF16))
            BPB = _e(nc.sbuf_tensor("b_p", [128, 4, 512], BF16))
            BDA = _e(nc.sbuf_tensor("b_da", [128, 2, 512], F32))
            BHL = _e(nc.sbuf_tensor("b_dhl", [128, 2, 512], BF16))
            BRHI = _e(nc.sbuf_tensor("b_rhi", [1, 512], BF16))
            BRLO = _e(nc.sbuf_tensor("b_rlo", [1, 512], BF16))
            BDRS = _e(nc.sbuf_tensor("b_drs", [128, TOK], F32))
            BCX = _e(nc.sbuf_tensor("b_cx", [128, 2, TOK], BF16))
            bslots = [PS0[:, j * 512:(j + 1) * 512] for j in range(4)]
            bpairs = [PS0[:, 0:1024], PS0[:, 1024:2048]]
            bcps = PS1[:, 0:TOK]
            bdnp = PS1[:, TOK:SEQ]

            fsb = {"qbuf": BQB, "qfree": [bar7, bar7], "sc_ps": bslots,
                   "sc_free": [bar7] * 4, "p_sb": BPB,
                   "p_free": [[bar7], [bar7], [bar7], [bar7]],
                   "pi": 0, "dacc": BDA, "dacc_free": [bar7, bar7],
                   "dn_ps": bdnp, "dn_free": bar7, "dr_sb": BDRS,
                   "dr_sb_free": bar7, "dhl": BHL, "dhl_free": bar7,
                   "rr": [BDRS[0:1, :], BRHI, BRLO], "rr_free": bar7,
                   "sc_pair": bpairs,
                   "p_pair": [BPB[:, 0:2, :].rearrange("p a b -> p (a b)"),
                              BPB[:, 2:4, :].rearrange("p a b -> p (a b)")],
                   "ctx_ps": bcps, "ctx_sb": BCX,
                   "co_free": [bar7, bar7], "ctx_gate": bar7}

            kv_gate = bar7
            for g in range(NKV):
                tk = p.dma_emit("sync", lambda e, g=g: e.dma_start_transpose(
                    BKT[:].rearrange("p c w -> p (c w)"),
                    kvg[0, g].rearrange("a b -> (a b)").rearrange(
                        "(t d) -> t d", d=128)),
                    waits=[gst["done"], kv_gate])
                tv = p.dma("sync", BVG[:],
                           kvg[1, g].rearrange("(c b) (z d) -> (b z) c d",
                                               b=8, d=128),
                           waits=[gst["done"], kv_gate])
                kv_gate = flash(
                    fsb, [4 * g + i for i in range(4)],
                    lambda t: BKT[:, t, :], lambda t: BVG[:, t, :],
                    qb_sp, qb_ready, [tk, tv], [tk, tv], halves_b, None,
                    ctx_sp, ctxb_ready)

        bar8 = p.snap()
        res3 = wo_residual("wob", wob, ctxb_ready, res2, bar=bar8)
        if DEBUG_DUMP == 3:
            dump_and_finish(res3)
            p.build(ctx)
            return nc

        bar9 = p.snap()
        res4 = ffn("f4", res3, bar=bar9)

        dump_and_finish(res4)
        p.build(ctx)
    return nc


def _host_tables():
    inv = 1.0 / (10000.0 ** (np.arange(0, HD, 2, dtype=np.float64) / HD))
    pos = np.arange(SEQ, dtype=np.float64)
    ang = np.outer(inv, pos)
    emb = np.concatenate([ang, ang], axis=0)
    return np.cos(emb), np.sin(emb)


def _core_positions(core):
    b = core // 2
    if core % 2 == 0:
        own = np.r_[0:512, 1536:2048]
    else:
        own = np.r_[512:1536]
    oth = np.setdiff1d(np.arange(SEQ), own)
    return b, own, oth


def _tri_masks():
    m = np.zeros((4, 128, 512), dtype=np.float32)
    for j in range(4):
        m[j] = (j * 128 + np.arange(128)[:, None] <= np.arange(512)[None, :])
    return m.astype(ml_dtypes.bfloat16)


def _mscal_for(core):
    v = np.zeros(8, dtype=np.float32)
    v[0:4] = 0.0 if core % 2 == 0 else 1.0
    v[4:8] = 1.0 if core % 2 == 0 else 0.0
    return np.broadcast_to(v, (128, 8)).copy()


def _check_sched(core, ownpos, locpos):
    tri = _tri_masks().astype(np.float32)
    ms = _mscal_for(core)[0]
    for half in range(2):
        qpos = ownpos[half * 512:(half + 1) * 512]
        cover = np.zeros((SEQ, 512), dtype=np.float64)
        for s, tile in enumerate(SCHED_A[half]):
            slot = MSLOT_A[half][s]
            tloc = np.arange(tile * 128, (tile + 1) * 128)
            if slot is None:
                keep = np.ones((128, 512))
            elif slot[0] == "tri":
                keep = tri[slot[1]]
            else:
                keep = np.full((128, 512), ms[slot[1]])
            cover[tloc[:, None], np.arange(512)[None, :]] += keep
        want = (locpos[:, None] <= qpos[None, :]).astype(np.float64)
        assert (cover == want).all(), f"flash-a schedule mismatch core {core}"


_NC_CACHE = {}


def kernel(L0_attn_a_hidden_states, L0_attn_a_Wq, L0_attn_a_Wk, L0_attn_a_Wv,
           L0_attn_a_Wo, L0_attn_b_Wq, L0_attn_b_Wo, L0_attn_b_kv_cache,
           L0_attn_b_block_table, L0_ffn_A, L0_ffn_B, L0_ffn_Wr, L0_ffn_br):
    x = np.asarray(L0_attn_a_hidden_states, dtype=np.float32)
    f32 = lambda a: np.ascontiguousarray(np.asarray(a, dtype=np.float32))
    consts = {
        "wqaT": f32(np.asarray(L0_attn_a_Wq).T),
        "wkT": f32(np.asarray(L0_attn_a_Wk).T),
        "wvT": f32(np.asarray(L0_attn_a_Wv).T),
        "woaT": f32(np.asarray(L0_attn_a_Wo).T),
        "wqb": f32(L0_attn_b_Wq),
        "wob": f32(L0_attn_b_Wo),
        "aT": f32(np.asarray(L0_ffn_A).reshape(128, D).T),
        "bbig": f32(np.asarray(L0_ffn_B).transpose(0, 2, 1).reshape(128, D)),
        "wrT": f32(np.asarray(L0_ffn_Wr).T),
        "br": f32(np.asarray(L0_ffn_br).reshape(8, 1)),
        "kvc": f32(np.asarray(L0_attn_b_kv_cache)
                   .transpose(1, 2, 0, 3, 4).reshape(2, NKV, NBLK, BLKSZ * HD)),
        "identF": np.eye(128, dtype=np.float32),
        "identB": np.eye(128, dtype=ml_dtypes.bfloat16),
        "onesB": np.ones((128, 1), dtype=ml_dtypes.bfloat16),
        "onesRF": np.ones((1, 128), dtype=np.float32),
        "onesRB": np.ones((1, 128), dtype=ml_dtypes.bfloat16),
    }
    bt = np.asarray(L0_attn_b_block_table, dtype=np.int32)
    consts["btc"] = np.ascontiguousarray(bt.reshape(4, 128).T).view(np.uint32)
    consts["e8"] = (np.kron(np.eye(8), np.ones((1, 16))) * 16.0).astype(
        ml_dtypes.bfloat16)
    rotTm = np.zeros((128, 128), dtype=np.float32)
    for i in range(64):
        rotTm[64 + i, i] = -1.0
        rotTm[i, 64 + i] = 1.0
    consts["rotT"] = rotTm.astype(ml_dtypes.bfloat16)

    cosg, sing = _host_tables()
    in_maps = []
    metas = []
    for core in range(8):
        b, own, oth = _core_positions(core)
        loc = np.concatenate([own, oth])
        if core < 2:
            _check_sched(core, own, loc)
        m = dict(consts)
        m["xT"] = np.ascontiguousarray(x[b].T[:, loc])
        m["cosf"] = np.ascontiguousarray(cosg[:, loc]).astype(ml_dtypes.bfloat16)
        m["sinf"] = np.ascontiguousarray(sing[:, loc]).astype(ml_dtypes.bfloat16)
        m["coso"] = np.ascontiguousarray(cosg[:, own]).astype(ml_dtypes.bfloat16)
        m["sino"] = np.ascontiguousarray(sing[:, own]).astype(ml_dtypes.bfloat16)
        m["masks"] = _tri_masks()
        m["mscal"] = _mscal_for(core)
        in_maps.append(m)
        metas.append((b, own))

    if "nc" not in _NC_CACHE:
        _NC_CACHE["nc"] = build_program()
    nc = _NC_CACHE["nc"]
    res = run_bass_kernel_spmd(nc, in_maps, core_ids=list(range(8)))

    out = np.zeros((4, SEQ, D), dtype=np.float32)
    key = "dbg" if DEBUG_DUMP else "outT"
    for core in range(8):
        b, own = metas[core]
        out[b, own, :] = res.results[core][key].T
    return out


# version 06
# speedup vs baseline: 1.6340x; 1.0891x over previous
import math
import numpy as np
import ml_dtypes
import concourse.bass as bass
import concourse.mybir as mybir
from concourse.bass_core_rust import IndirectOffsetOnAxis
from concourse.bass_utils import run_bass_kernel_spmd

F32 = mybir.dt.float32
BF16 = mybir.dt.bfloat16
U32 = mybir.dt.uint32
AF = mybir.ActivationFunctionType
ALU = mybir.AluOpType
AX = mybir.AxisListType

D = 4096
NCHUNK = 32
TOK = 1024
SEQ = 2048
NH = 32
NKV = 8
HD = 128
NBLK = 1024
BLKSZ = 16
NPAGES = 512
INVSQ = 1.0 / math.sqrt(128.0)
CSTAB = 12.0
EPS = 1e-5

SCHED_A = [[8, 9, 10, 11, 0, 1, 2, 3], list(range(16))]
MSLOT_A = [
    [("scal", 0), ("scal", 1), ("scal", 2), ("scal", 3),
     ("tri", 0), ("tri", 1), ("tri", 2), ("tri", 3)],
    [None, None, None, None,
     ("tri", 0), ("tri", 1), ("tri", 2), ("tri", 3),
     None, None, None, None,
     ("scal", 4), ("scal", 5), ("scal", 6), ("scal", 7)],
]

import os
DEBUG_DUMP = int(os.environ.get("KDBG", "0"))
QPN = int(os.environ.get("KQPN", "16"))
KROPE = int(os.environ.get("KROPE", "0"))


class P:

    ENG = ("sync", "scalar", "vector", "tensor", "gpsimd")

    def __init__(self, nc):
        self.nc = nc
        self.ops = {e: [] for e in self.ENG}
        self.sems = {}
        self.cnt = {}
        self.lastw = {}

    def sem(self, name, ctx):
        self.sems[name] = ctx.enter_context(self.nc.semaphore(name))
        self.cnt[name] = 0
        return name

    def emit(self, eng, fn, waits=(), inc=None, n=1):
        wl = []
        flat = []
        for t in waits:
            if t is None:
                continue
            if isinstance(t, list):
                flat.extend(x for x in t if x is not None)
            else:
                flat.append(t)
        for t in flat:
            s, v = t
            k = (eng, s)
            if self.lastw.get(k, 0) < v:
                self.lastw[k] = v
                wl.append((s, v))
        tok = None
        if inc is not None:
            self.cnt[inc] += n
            tok = (inc, self.cnt[inc])
        self.ops[eng].append((wl, fn, inc, n))
        return tok

    def init_lanes(self, ctx, spec):
        self.lanes = {}
        self.lane_i = {}
        for eng, n in spec.items():
            self.lanes[eng] = [self.sem(f"dq_{eng}{i}", ctx) for i in range(n)]
            self.lane_i[eng] = 0

    def dma_emit(self, eng, fn, waits=()):
        lane = self.lanes[eng][self.lane_i[eng] % len(self.lanes[eng])]
        self.lane_i[eng] += 1
        base = self.cnt[lane]
        w = list(waits) + ([(lane, base)] if base else [])
        return self.emit(eng, fn, waits=w, inc=lane, n=16)

    def snap(self):
        return [(k, v) for (k, v) in self.cnt.items() if v > 0]

    def dma(self, eng, out, in_, waits=(), **kw):
        return self.dma_emit(
            eng, lambda e: e.dma_start(out=out, in_=in_, **kw), waits=waits)

    def build(self, ctx):
        block = ctx.enter_context(self.nc.Block())
        prog = self

        def runner(ename):
            def run(e):
                for (wl, fn, inc, n) in prog.ops[ename]:
                    for (s, v) in wl:
                        e.wait_ge(prog.sems[s], v)
                    ins = fn(e)
                    if inc is not None:
                        ins.then_inc(prog.sems[inc], n)
            return run

        block.sync(runner("sync"))
        block.scalar(runner("scalar"))
        block.vector(runner("vector"))
        block.tensor(runner("tensor"))
        block.gpsimd(runner("gpsimd"))


def cdiv(a, b):
    return (a + b - 1) // b


def build_program():
    nc = bass.Bass("TRN2", target_bir_lowering=False)
    dt = nc.dram_tensor
    xT = dt("xT", [D, SEQ], F32, kind="ExternalInput").ap()
    wqaT = dt("wqaT", [D, D], F32, kind="ExternalInput").ap()
    wkT = dt("wkT", [D, NKV * HD], F32, kind="ExternalInput").ap()
    wvT = dt("wvT", [D, NKV * HD], F32, kind="ExternalInput").ap()
    woaT = dt("woaT", [D, D], F32, kind="ExternalInput").ap()
    wqb = dt("wqb", [D, D], F32, kind="ExternalInput").ap()
    wob = dt("wob", [D, D], F32, kind="ExternalInput").ap()
    aT = dt("aT", [D, 128], F32, kind="ExternalInput").ap()
    bbig = dt("bbig", [128, D], F32, kind="ExternalInput").ap()
    wrT = dt("wrT", [D, 8], F32, kind="ExternalInput").ap()
    brD = dt("br", [8, 1], F32, kind="ExternalInput").ap()
    kvc = dt("kvc", [2, NKV, NBLK, BLKSZ * HD], F32, kind="ExternalInput").ap()
    btc = dt("btc", [128, 4], U32, kind="ExternalInput").ap()
    cosf = dt("cosf", [128, SEQ], BF16, kind="ExternalInput").ap()
    sinf = dt("sinf", [128, SEQ], BF16, kind="ExternalInput").ap()
    coso = dt("coso", [128, TOK], BF16, kind="ExternalInput").ap()
    sino = dt("sino", [128, TOK], BF16, kind="ExternalInput").ap()
    identF = dt("identF", [128, 128], F32, kind="ExternalInput").ap()
    identB = dt("identB", [128, 128], BF16, kind="ExternalInput").ap()
    rotT = dt("rotT", [128, 128], BF16, kind="ExternalInput").ap()
    e8 = dt("e8", [8, 128], BF16, kind="ExternalInput").ap()
    onesB = dt("onesB", [128, 1], BF16, kind="ExternalInput").ap()
    onesRF = dt("onesRF", [1, 128], F32, kind="ExternalInput").ap()
    onesRB = dt("onesRB", [1, 128], BF16, kind="ExternalInput").ap()
    masksD = dt("masks", [4, 128, 512], BF16, kind="ExternalInput").ap()
    mscalD = dt("mscal", [128, 8], F32, kind="ExternalInput").ap()
    outT = dt("outT", [D, TOK], F32, kind="ExternalOutput").ap()

    xn1 = dt("xn1", [D, SEQ], BF16, kind="Internal").ap()
    xn3 = dt("xn3", [D, TOK], BF16, kind="Internal").ap()
    kvg = dt("kvg", [2, NKV, NPAGES, BLKSZ * HD], BF16, kind="Internal").ap()
    qa_sp = dt("qa_sp", [NH, 128, TOK], BF16, kind="Internal").ap()
    qb_sp = dt("qb_sp", [NH, 128, TOK], BF16, kind="Internal").ap()
    ctx_sp = dt("ctx_sp", [NH, 128, TOK], BF16, kind="Internal").ap()

    dbg = dt("dbg", [D, TOK], F32, kind="ExternalOutput").ap() if DEBUG_DUMP else None

    from contextlib import ExitStack
    ctx = ExitStack()
    with ctx:
        E = ctx.enter_context
        p = P(nc)
        sSY = p.sem("sSY", ctx)
        sSC = p.sem("sSC", ctx)
        sGP = p.sem("sGP", ctx)
        sPE = p.sem("sPE", ctx)
        sAC = p.sem("sAC", ctx)
        sVE = p.sem("sVE", ctx)
        sPO = p.sem("sPO", ctx)
        p.init_lanes(ctx, {"sync": 8, "scalar": 6, "gpsimd": 6})

        PS0 = E(nc.psum_tensor("PS0", [128, SEQ], F32))
        PS1 = E(nc.psum_tensor("PS1", [128, SEQ], F32))

        TR = E(nc.sbuf_tensor("TR", [128, NCHUNK, TOK], F32))
        IDF = E(nc.sbuf_tensor("IDF", [128, 128], F32))
        IDB = E(nc.sbuf_tensor("IDB", [128, 128], BF16))
        ROT = E(nc.sbuf_tensor("ROT", [128, 128], BF16))
        E8S = E(nc.sbuf_tensor("E8S", [8, 128], BF16))
        ONE = E(nc.sbuf_tensor("ONE", [128, 1], BF16))
        ONEF = E(nc.sbuf_tensor("ONEF", [128, 1], F32))
        ONR = E(nc.sbuf_tensor("ONR", [1, 128], F32))
        ONRB = E(nc.sbuf_tensor("ONRB", [1, 128], BF16))
        BRS = E(nc.sbuf_tensor("BRS", [8, 1], F32))
        BTS = E(nc.sbuf_tensor("BTS", [128, 4], U32))

        tc_last = []
        for (dst, src) in ((IDF, identF), (IDB, identB), (ROT, rotT),
                           (E8S, e8), (ONE, onesB), (ONR, onesRF),
                           (ONRB, onesRB), (BRS, brD), (BTS, btc)):
            tc_last.append(p.dma("sync", dst[:], src))
        t_const = tc_last
        EPSC = E(nc.sbuf_tensor("EPSC", [128, 1], F32))
        CSTB = E(nc.sbuf_tensor("CSTB", [128, 1], F32))
        p.emit("vector", lambda e: e.memset(EPSC[:], EPS), inc=sVE)
        p.emit("vector", lambda e: e.memset(CSTB[:], -CSTAB), inc=sVE)
        t_onef = p.emit("vector", lambda e: e.memset(ONEF[:], 1.0), inc=sVE)

        from contextlib import ExitStack as _ESg
        gstg_ctx = _ESg()
        GSTG = gstg_ctx.enter_context(
            nc.sbuf_tensor("GSTG", [128, 1, BLKSZ * HD], BF16))
        gjobs = [(kv, h, c) for kv in range(2) for h in range(NKV)
                 for c in range(4)]
        gst = {"i": 0, "free": [None, None], "done": None}

        def emit_gathers(njobs):
            for _ in range(njobs):
                i = gst["i"]
                if i >= len(gjobs):
                    return
                kv, h, c = gjobs[i]
                slot = 0
                stg = GSTG[:, 0, :]

                def g_fn(e, kv=kv, h=h, c=c, stg=stg):
                    flat = kvc.rearrange("a b c d -> (a b c) d")
                    return e.indirect_dma_start(
                        stg, None, flat,
                        IndirectOffsetOnAxis(BTS[:, c:c + 1], 0),
                        element_offset=(kv * NKV + h) * NBLK * (BLKSZ * HD))
                tg = p.dma_emit("gpsimd", g_fn,
                                waits=[t_const, gst["free"][slot]])
                ts = p.dma("sync", kvg[kv, h, c * 128:(c + 1) * 128, :], stg,
                           waits=[tg])
                gst["free"][slot] = ts
                gst["done"] = ts
                gst["i"] = i + 1

        emit_gathers(8)

        def ln_rows(N, t_stats, psA, psB, rows, reps):
            sum_ps = psA[0:1, :]
            sq_ps = psB[0:1, :]
            reps_ps = (psA, psB)
            """rows: 3 SBUF [1, N] slots (m, msq, var/r in place)."""
            m_row, msq_row, var_row = rows
            r_row = var_row
            t1 = p.emit("scalar", lambda e: e.activation(
                m_row, sum_ps, AF.Copy, scale=1.0 / D), waits=[t_stats], inc=sAC)
            t2 = p.emit("scalar", lambda e: e.activation(
                msq_row, sq_ps, AF.Copy, scale=1.0 / D), waits=[t_stats, t1],
                inc=sAC)
            t3 = p.emit("vector", lambda e: e.tensor_tensor(
                var_row, m_row, m_row, ALU.mult), waits=[t1], inc=sVE)
            t4 = p.emit("vector", lambda e: e.tensor_tensor(
                var_row, msq_row, var_row, ALU.subtract), waits=[t2, t3],
                inc=sVE)
            t5 = p.emit("scalar", lambda e: e.activation(
                r_row, var_row, AF.Ln, bias=EPSC[0:1, :]),
                waits=[t4, t_onef], inc=sAC)
            t6 = p.emit("scalar", lambda e: e.activation(
                r_row, r_row, AF.Exp, scale=-0.5), waits=[t5], inc=sAC)
            tm = tr = None
            for j in range(cdiv(N, 512)):
                sl = slice(j * 512, (j + 1) * 512)
                tm = p.emit("tensor", lambda e, sl=sl: e.matmul(
                    reps_ps[0][:, sl], ONR[:], m_row[:, sl], start=True,
                    stop=True), waits=[t1, t2, t_onef], inc=sPE)
                tr = p.emit("tensor", lambda e, sl=sl: e.matmul(
                    reps_ps[1][:, sl], ONR[:], r_row[:, sl], start=True,
                    stop=True), waits=[t6], inc=sPE)
            t_mrep = p.emit("scalar", lambda e: e.copy(reps[0], reps_ps[0]),
                            waits=[tm, tr], inc=sAC)
            t_rrep = p.emit("scalar", lambda e: e.copy(reps[1], reps_ps[1]),
                            waits=[tr], inc=sAC)
            return t_mrep, t_rrep

        from contextlib import ExitStack as _ES
        with _ES() as p1ctx:
            _e = p1ctx.enter_context
            xc = _e(nc.sbuf_tensor("p1_xc", [128, 2, SEQ], F32))
            xbf = _e(nc.sbuf_tensor("p1_xb", [128, 2, SEQ], BF16))
            rows1 = _e(nc.sbuf_tensor("p1_rows", [1, 3, SEQ], F32))
            reps1 = _e(nc.sbuf_tensor("p1_reps", [128, 2, SEQ], F32))
            xnb = _e(nc.sbuf_tensor("p1_xn", [128, 2, SEQ], BF16))
            mrep_ps = PS0[:, :]
            rrep_ps = PS1[:, :]
            sum_ps = mrep_ps
            sq_ps = rrep_ps

            ld_gate = [None, None]
            pe_gate = [None, None]
            t_stats = None
            trunk_init = {}
            for c in range(NCHUNK):
                sl = c % 2
                tl = p.dma("sync", xc[:, sl, :], xT[c * 128:(c + 1) * 128, :],
                           waits=[ld_gate[sl]])
                trunk_init[c] = p.emit("vector", lambda e, c=c, sl=sl: e.tensor_copy(
                    TR[:, c, :], xc[:, sl, 0:TOK]), waits=[tl], inc=sVE)
                tb = p.emit("vector", lambda e, sl=sl: e.tensor_copy(
                    xbf[:, sl, :], xc[:, sl, :]), waits=[tl, pe_gate[sl]],
                    inc=sVE)
                tsum = None
                for j in range(4):
                    s5 = slice(j * 512, (j + 1) * 512)
                    tsum = p.emit("tensor", lambda e, sl=sl, s5=s5, c=c: e.matmul(
                        sum_ps[0:1, s5], ONE[:], xbf[:, sl, s5],
                        start=(c == 0), stop=(c == NCHUNK - 1)),
                        waits=[tb, t_const], inc=sPE)
                tq = p.emit("scalar", lambda e, sl=sl: e.activation(
                    xbf[:, sl, :], xbf[:, sl, :], AF.Square),
                    waits=[tsum], inc=sAC)
                for j in range(4):
                    s5 = slice(j * 512, (j + 1) * 512)
                    t_stats = p.emit("tensor", lambda e, sl=sl, s5=s5, c=c: e.matmul(
                        sq_ps[0:1, s5], ONE[:], xbf[:, sl, s5],
                        start=(c == 0), stop=(c == NCHUNK - 1)),
                        waits=[tq], inc=sPE)
                ld_gate[sl] = tb
                pe_gate[sl] = t_stats
                if c % 8 == 3:
                    emit_gathers(2)

            rows = [rows1[:, i, :] for i in range(3)]
            t_mrep, t_rrep = ln_rows(SEQ, t_stats, mrep_ps, rrep_ps,
                                     rows, (reps1[:, 0, :], reps1[:, 1, :]))

            ld2 = list(ld_gate)
            st2 = [None, None]
            xn1_done = None
            for c in range(NCHUNK):
                sl = c % 2
                tl = p.dma("scalar", xc[:, sl, :], xT[c * 128:(c + 1) * 128, :],
                           waits=[ld2[sl]])
                td = p.emit("vector", lambda e, sl=sl: e.tensor_tensor(
                    xc[:, sl, :], xc[:, sl, :], reps1[:, 0, :], ALU.subtract),
                    waits=[tl, t_mrep, t_rrep], inc=sVE)
                tn = p.emit("vector", lambda e, sl=sl: e.tensor_tensor(
                    xnb[:, sl, :], xc[:, sl, :], reps1[:, 1, :], ALU.mult),
                    waits=[td, st2[sl]], inc=sVE)
                ld2[sl] = tn
                ts = p.dma("sync", xn1[c * 128:(c + 1) * 128, :], xnb[:, sl, :],
                           waits=[tn])
                st2[sl] = ts
                xn1_done = ts
                if c % 8 == 3:
                    emit_gathers(2)

        def q_production(tag, wT_dram, xn_dram, q_spill, t_xn_ready,
                         gather_hook=None, bar=None):
            spills = {}
            bar = bar or []
            from contextlib import ExitStack as _ES
            with _ES() as qpctx:
                _e = qpctx.enter_context
                ws = _e(nc.sbuf_tensor(f"{tag}_ws", [128, 2, NCHUNK, 256], BF16))
                xcq = _e(nc.sbuf_tensor(f"{tag}_xc", [128, 8, TOK], BF16))
                qbr = _e(nc.sbuf_tensor(f"{tag}_qb", [128, TOK], BF16))
                t1s = _e(nc.sbuf_tensor(f"{tag}_t1", [128, TOK], F32))
                t2s = _e(nc.sbuf_tensor(f"{tag}_t2", [128, TOK], F32))
                qo = _e(nc.sbuf_tensor(f"{tag}_qo", [128, 2, TOK], BF16))
                cs = _e(nc.sbuf_tensor(f"{tag}_cs", [128, 2, TOK], BF16))
                tcs = p.dma("sync", cs[:, 0, :], coso, waits=[bar])
                tsn = p.dma("sync", cs[:, 1, :], sino, waits=[bar])
                qreg = [[PS0[:, 0:TOK], PS0[:, TOK:SEQ]],
                        [PS1[:, 0:TOK], PS1[:, TOK:SEQ]]]
                wfree = [bar, bar]
                xfree = [bar, bar]
                qofree = [bar, bar]
                st = {"last_rot": bar, "last_qbr": bar, "last_tv3": bar}
                region_free = {(-2, 0): bar, (-2, 1): bar, (-1, 0): bar,
                               (-1, 1): bar}
                mm_of = {}

                def emit_rope(ps):
                    qps = qreg[ps % 2]
                    for hh in range(2):
                        h = ps * 2 + hh
                        slq = hh
                        te = p.emit("scalar", lambda e, qps=qps, hh=hh: e.copy(
                            qbr[:], qps[hh]),
                            waits=[mm_of[(ps, hh)], st["last_rot"],
                                   st["last_qbr"]], inc=sAC)
                        trm = None
                        for j in range(2):
                            s5 = slice(j * 512, (j + 1) * 512)
                            trm = p.emit("tensor", lambda e, s5=s5, qps=qps, hh=hh: e.matmul(
                                qps[hh][:, s5], ROT[:], qbr[:, s5],
                                start=True, stop=True),
                                waits=[te, st["last_rot"]], inc=sPE)
                        st["last_qbr"] = trm
                        tv1 = p.emit("vector", lambda e: e.tensor_tensor(
                            t1s[:], qbr[:], cs[:, 0, :], ALU.mult),
                            waits=[te, tcs, st["last_tv3"]], inc=sVE)
                        tv2 = p.emit("vector", lambda e, qps=qps, hh=hh: e.tensor_tensor(
                            t2s[:], qps[hh], cs[:, 1, :], ALU.mult),
                            waits=[trm, tsn, tv1], inc=sVE)
                        tv3 = p.emit("vector", lambda e, slq=slq: e.tensor_tensor(
                            qo[:, slq, :], t1s[:], t2s[:], ALU.add),
                            waits=[tv2, qofree[slq]], inc=sVE)
                        st["last_tv3"] = tv3
                        st["last_rot"] = tv2
                        region_free[(ps, hh)] = tv2
                        tsp = p.dma("scalar", q_spill[h], qo[:, slq, :],
                                    waits=[tv3])
                        qofree[slq] = tsp
                        spills[h] = tsp

                for ps in range(QPN):
                    slw = ps % 2
                    qps = qreg[ps % 2]
                    if gather_hook:
                        gather_hook(3)
                    tw = p.dma("gpsimd", ws[:, slw, :, :],
                               wT_dram[:, ps * 256:(ps + 1) * 256].rearrange(
                                   "(c p) w -> p c w", p=128),
                               waits=[wfree[slw]])
                    mm_tok = [None, None]
                    for c4 in range(NCHUNK // 4):
                        hb = c4 % 2
                        tl = p.dma(
                            "sync", xcq[:, hb * 4:hb * 4 + 4, :],
                            xn_dram[c4 * 512:(c4 + 1) * 512, 0:TOK].rearrange(
                                "(c p) t -> p c t", p=128),
                            waits=[t_xn_ready, xfree[hb]])
                        for cs4 in range(4):
                            c = c4 * 4 + cs4
                            for hh in range(2):
                                wmm = [tl, tw, region_free[(ps - 2, hh)]]
                                for j in range(2):
                                    s5 = slice(j * 512, (j + 1) * 512)
                                    mm_tok[hh] = p.emit(
                                        "tensor",
                                        lambda e, hh=hh, c=c, s5=s5, cs4=cs4, hb=hb, slw=slw, qps=qps: e.matmul(
                                            qps[hh][:, s5],
                                            ws[:, slw, c, hh * 128:(hh + 1) * 128],
                                            xcq[:, hb * 4 + cs4, s5],
                                            start=(c == 0), stop=(c == NCHUNK - 1)),
                                        waits=wmm, inc=sPE)
                        xfree[hb] = mm_tok[1]
                    wfree[slw] = mm_tok[1]
                    mm_of[(ps, 0)] = mm_tok[0]
                    mm_of[(ps, 1)] = mm_tok[1]
                    if ps > 0:
                        emit_rope(ps - 1)
                emit_rope(QPN - 1)
            return spills

        def flash(s, qheads, kt_fn, v_fn, q_spill, q_ready, t_k, t_v,
                  halves, msk, ctx_dst, ctx_ready, t_msk=None):
            LOOK = 2
            for qh in qheads:
                tql = p.dma("sync", s["qbuf"][:, qh % 2, :], q_spill[qh],
                            waits=[q_ready[qh], s["qfree"][qh % 2]])
                ctx_tok = None
                pend = []
                prev_d = {}
                dacc_of = {}

                def consume():
                    nonlocal ctx_tok
                    (half, si, nst, tile, slot, pp, scb, pb, cols, tmm) = \
                        pend.pop(0)
                    (half2, si2, nst2, tile2, slot2, pp2, scb2, pb2, cols2,
                     tmm2) = pend.pop(0)
                    assert pp2 == pp + 1 and pp % 2 == 0
                    tex = p.emit("scalar",
                                 lambda e, pp=pp: e.activation(
                                     s["p_pair"][pp // 2],
                                     s["sc_pair"][pp // 2], AF.Exp,
                                     bias=CSTB[:], scale=INVSQ),
                                 waits=[tmm, tmm2, t_onef] + s["p_free"][pp]
                                 + s["p_free"][pp2], inc=sAC)
                    s["sc_free"][pp] = tex
                    s["sc_free"][pp2] = tex
                    assert half == half2
                    tcons_l = []
                    tav_l = []
                    for (hf, si_, nst_, tile_, slot_, pp_, pb_) in (
                            (half, si, nst, tile, slot, pp, pb),
                            (half2, si2, nst2, tile2, slot2, pp2, pb2)):
                        tcons = tex
                        if slot_ is not None:
                            kind, j = slot_
                            if kind == "tri":
                                tcons = p.emit(
                                    "vector",
                                    lambda e, pb_=pb_, j=j: e.tensor_tensor(
                                        pb_, pb_, msk[0][:, j, :], ALU.mult),
                                    waits=[tex, t_msk], inc=sVE)
                            else:
                                tcons = p.emit(
                                    "vector",
                                    lambda e, pb_=pb_, j=j: e.tensor_scalar(
                                        pb_, pb_, msk[1][:, j:j + 1], None,
                                        ALU.mult),
                                    waits=[tex, t_msk], inc=sVE)
                        tcons_l.append(tcons)
                        tav = p.emit(
                            "tensor",
                            lambda e, tile_=tile_, cols=cols, pb_=pb_, si_=si_, nst_=nst_: e.matmul(
                                s["ctx_ps"][:, cols], v_fn(tile_), pb_,
                                start=(si_ == 0), stop=(si_ == nst_ - 1)),
                            waits=[tcons, t_v, s["ctx_gate"]], inc=sPE)
                        tav_l.append(tav)
                    ctx_tok = tav_l[-1]
                    dacc = dacc_of[half]
                    dk = s["pi"] % 2
                    tmpb = s["dhl"][:, dk, :]
                    tps = p.emit("vector", lambda e, tmpb=tmpb, a=pb, b=pb2: e.tensor_tensor(
                        tmpb, a, b, ALU.add),
                        waits=tcons_l + [s["dhl_free"][dk]], inc=sVE)
                    td = p.emit("vector", lambda e, dacc=dacc, tmpb=tmpb: e.tensor_tensor(
                        dacc, dacc, tmpb, ALU.add),
                        waits=[tps, prev_d[half]], inc=sVE)
                    prev_d[half] = td
                    s["dhl_free"][dk] = td
                    s["p_free"][pp] = [tav_l[0], tps]
                    s["p_free"][pp2] = [tav_l[1], tps]

                for half, (cols, steps) in enumerate(halves):
                    assert cols.stop - cols.start == 512
                    dacc = s["dacc"][:, half, :]
                    dacc_of[half] = dacc
                    tz = p.emit("vector", lambda e, dacc=dacc: e.memset(
                        dacc, 0.0), waits=[s["dacc_free"][half]], inc=sVE)
                    prev_d[half] = tz
                    for si, (tile, slot) in enumerate(steps):
                        pp = s["pi"] % 4
                        s["pi"] += 1
                        scb = s["sc_ps"][pp]
                        tmm = p.emit(
                            "tensor",
                            lambda e, tile=tile, cols=cols, scb=scb, qh=qh: e.matmul(
                                scb, kt_fn(tile),
                                s["qbuf"][:, qh % 2, cols],
                                start=True, stop=True),
                            waits=[tql, t_k, s["sc_free"][pp]], inc=sPE)
                        pend.append((half, si, len(steps), tile, slot, pp,
                                     scb, s["p_sb"][:, pp, :], cols, tmm))
                        if len(pend) >= 4:
                            consume()
                while pend:
                    consume()
                denom_mm = []
                for half, (cols, steps) in enumerate(halves):
                    dacc = dacc_of[half]
                    dhi = s["dhl"][:, 0, :]
                    dlo = s["dhl"][:, 1, :]
                    th1 = p.emit("vector", lambda e, dacc=dacc, dhi=dhi: e.tensor_copy(
                        dhi, dacc),
                        waits=[prev_d[half]] + s["dhl_free"], inc=sVE)
                    th2 = p.emit("vector", lambda e, dacc=dacc, dhi=dhi, dlo=dlo: e.tensor_tensor(
                        dlo, dacc, dhi, ALU.subtract), waits=[th1], inc=sVE)
                    td1 = p.emit("tensor", lambda e, cols=cols, dhi=dhi: e.matmul(
                        s["dn_ps"][0:1, cols], ONE[:], dhi,
                        start=True, stop=False),
                        waits=[th1, t_const, s["dn_free"]], inc=sPE)
                    td2 = p.emit("tensor", lambda e, cols=cols, dlo=dlo: e.matmul(
                        s["dn_ps"][0:1, cols], ONE[:], dlo,
                        start=False, stop=True), waits=[th2], inc=sPE)
                    denom_mm.append(td2)
                    s["dacc_free"][half] = th2
                    s["dhl_free"] = [td1, td2]
                rrow = s["rr"]
                trr = None
                last_split = s["rr_free"]
                for j in range(2):
                    s5 = slice(j * 512, (j + 1) * 512)
                    trc = p.emit("vector", lambda e, s5=s5: e.reciprocal(
                        rrow[0][0:1, 0:512], s["dn_ps"][0:1, s5]),
                        waits=denom_mm + [last_split, s["dr_sb_free"]],
                        inc=sVE)
                    th3 = p.emit("vector", lambda e: e.tensor_copy(
                        rrow[1][0:1, :], rrow[0][0:1, 0:512]), waits=[trc],
                        inc=sVE)
                    th4 = p.emit("vector", lambda e: e.tensor_tensor(
                        rrow[2][0:1, :], rrow[0][0:1, 0:512], rrow[1][0:1, :],
                        ALU.subtract), waits=[th3], inc=sVE)
                    p.emit("tensor", lambda e, s5=s5: e.matmul(
                        s["dn_ps"][:, s5], ONRB[:], rrow[1][0:1, :],
                        start=True, stop=False),
                        waits=[th3, th4, t_const], inc=sPE)
                    trr = p.emit("tensor", lambda e, s5=s5: e.matmul(
                        s["dn_ps"][:, s5], ONRB[:], rrow[2][0:1, :],
                        start=False, stop=True), waits=[th4], inc=sPE)
                    last_split = trr
                tre = p.emit("scalar", lambda e: e.copy(
                    s["dr_sb"][:], s["dn_ps"]),
                    waits=[trr, s["dr_sb_free"]], inc=sAC)
                s["dn_free"] = tre
                s["rr_free"] = trr
                tnm = p.emit("vector", lambda e, qh=qh: e.tensor_tensor(
                    s["ctx_sb"][:, qh % 2, :], s["ctx_ps"], s["dr_sb"][:],
                    ALU.mult), waits=[tre, ctx_tok, s["co_free"][qh % 2]],
                    inc=sVE)
                s["dr_sb_free"] = tnm
                s["ctx_gate"] = tnm
                tsp = p.dma("scalar", ctx_dst[qh], s["ctx_sb"][:, qh % 2, :],
                            waits=[tnm])
                s["co_free"][qh % 2] = tsp
                s["qfree"][qh % 2] = tsp
                ctx_ready[qh] = tsp
            return s["ctx_gate"]

        def wo_residual(tag, wo_dram, ctx_ready, trunk_gate, bar=None):
            res_tok = {}
            bar = bar or []
            from contextlib import ExitStack as _ES
            with _ES() as woctx:
                _e = woctx.enter_context
                wos = _e(nc.sbuf_tensor(f"{tag}_wo", [128, 2, NCHUNK, 256], BF16))
                cxs = _e(nc.sbuf_tensor(f"{tag}_cx", [128, 4, TOK], BF16))
                yps = [PS0[:, 0:TOK], PS0[:, TOK:SEQ], PS1[:, 0:TOK],
                       PS1[:, TOK:SEQ]]
                st_wo = {"psum_gate": [bar, bar],
                         "cfree": [bar, bar, bar, bar]}

                def _wo_grp(grp, tw):
                    slp = grp % 2
                    ypair = (yps[0], yps[1]) if slp == 0 else (yps[2], yps[3])
                    tmm = None
                    for h in range(NH):
                        slc = h % 4
                        tc = p.dma("sync", cxs[:, slc, :], ctx_sp[h],
                                   waits=[ctx_ready.get(h),
                                          st_wo["cfree"][slc]])
                        for j in range(2):
                            for q5 in range(2):
                                s5 = slice(q5 * 512, (q5 + 1) * 512)
                                tmm = p.emit(
                                    "tensor",
                                    lambda e, h=h, j=j, s5=s5, slc=slc, ypair=ypair: e.matmul(
                                        ypair[j][:, s5],
                                        wos[:, grp % 2, h, j * 128:(j + 1) * 128],
                                        cxs[:, slc, s5],
                                        start=(h == 0), stop=(h == NH - 1)),
                                    waits=[tw, tc, st_wo["psum_gate"][slp]],
                                    inc=sPE)
                        st_wo["cfree"][slc] = tmm
                    tev = None
                    for j in range(2):
                        oc = grp * 2 + j
                        tev = p.emit("vector", lambda e, j=j, oc=oc, ypair=ypair: e.tensor_tensor(
                            TR[:, oc, :], TR[:, oc, :], ypair[j], ALU.add),
                            waits=[tmm, trunk_gate.get(oc)], inc=sVE)
                        res_tok[oc] = tev
                    st_wo["psum_gate"][slp] = tev
                    return tev
                psum_gate = bar
                cfree = [bar, bar]
                wfree = [bar, bar]
                wtoks = {}
                for grp in range(16):
                    slw = grp % 2
                    if grp >= 2:
                        wfree[slw] = _wo_grp(grp - 2, wtoks[grp - 2])
                    wtoks[grp] = p.dma(
                        "gpsimd", wos[:, slw, :, :],
                        wo_dram[:, grp * 256:(grp + 1) * 256].rearrange(
                            "(c p) w -> p c w", p=128),
                        waits=[wfree[slw]])
                for gg in (14, 15):
                    _wo_grp(gg, wtoks[gg])
            return res_tok

        def ln_trunk(tag, trunk_gate, consume, pool_engine="vector",
                     bar=None):
            bar = bar or []
            from contextlib import ExitStack as _ES
            with _ES() as lnctx:
                _e = lnctx.enter_context
                xb2 = _e(nc.sbuf_tensor(f"{tag}_b", [128, 2, TOK], BF16))
                xq2 = _e(nc.sbuf_tensor(f"{tag}_q", [128, 2, TOK], BF16))
                rows2 = _e(nc.sbuf_tensor(f"{tag}_rows", [1, 3, TOK], F32))
                reps2 = _e(nc.sbuf_tensor(f"{tag}_reps", [128, 2, TOK], F32))
                xn2 = _e(nc.sbuf_tensor(f"{tag}_xn", [128, 2, TOK], BF16))
                dd2 = _e(nc.sbuf_tensor(f"{tag}_d", [128, 2, TOK], F32))
                mr2 = PS0[:, TOK:SEQ]
                rr2 = PS1[:, TOK:SEQ]
                pe_free = [bar, bar]
                t_stats = None
                for c in range(NCHUNK):
                    sl = c % 2
                    tb = p.emit(pool_engine, lambda e, c=c, sl=sl: e.tensor_copy(
                        xb2[:, sl, :], TR[:, c, :]),
                        waits=[trunk_gate.get(c), pe_free[sl]], inc=sPO)
                    tq = p.emit(pool_engine, lambda e, sl=sl: e.tensor_tensor(
                        xq2[:, sl, :], xb2[:, sl, :], xb2[:, sl, :], ALU.mult),
                        waits=[tb], inc=sPO)
                    for j in range(2):
                        s5 = slice(j * 512, (j + 1) * 512)
                        p.emit("tensor", lambda e, sl=sl, s5=s5, c=c: e.matmul(
                            mr2[0:1, s5], ONE[:], xb2[:, sl, s5],
                            start=(c == 0), stop=(c == NCHUNK - 1)),
                            waits=[tb, t_const], inc=sPE)
                        t_stats = p.emit("tensor", lambda e, sl=sl, s5=s5, c=c: e.matmul(
                            rr2[0:1, s5], ONE[:], xq2[:, sl, s5],
                            start=(c == 0), stop=(c == NCHUNK - 1)),
                            waits=[tq], inc=sPE)
                    pe_free[sl] = t_stats
                rows = [rows2[:, i, :] for i in range(3)]
                t_mrep, t_rrep = ln_rows(TOK, t_stats, mr2, rr2, rows,
                                         (reps2[:, 0, :], reps2[:, 1, :]))
                cons_tok = [bar, bar]
                for c in range(NCHUNK):
                    sl = c % 2
                    td = p.emit("vector", lambda e, c=c, sl=sl: e.tensor_tensor(
                        dd2[:, sl, :], TR[:, c, :], reps2[:, 0, :],
                        ALU.subtract),
                        waits=[trunk_gate.get(c), t_mrep, t_rrep, cons_tok[sl]],
                        inc=sVE)
                    tn = p.emit("vector", lambda e, sl=sl: e.tensor_tensor(
                        xn2[:, sl, :], dd2[:, sl, :], reps2[:, 1, :], ALU.mult),
                        waits=[td], inc=sVE)
                    cons_tok[sl] = consume(c, xn2[:, sl, :], tn)

        def dump_and_finish(gate):
            last = None
            tgt = dbg if DEBUG_DUMP else outT
            for c in range(NCHUNK):
                last = p.dma("sync", tgt[c * 128:(c + 1) * 128, :], TR[:, c, :],
                             waits=[gate.get(c)])
            tail = [last]
            for eng, lns in p.lanes.items():
                for ln in lns:
                    if p.cnt[ln]:
                        tail.append((ln, p.cnt[ln]))
            p.emit("sync", lambda e: e.nop(), waits=tail)

        if DEBUG_DUMP == 11:
            emit_gathers(len(gjobs))
            gstg_ctx.close()
            p.emit("vector", lambda e: e.engine_nop(),
                   waits=[xn1_done, gst["done"]], inc=sVE)
            dump_and_finish(trunk_init)
            p.build(ctx)
            return nc

        bar_p1 = p.snap()
        qa_ready = q_production("qa", wqaT, xn1, qa_sp, xn1_done,
                                gather_hook=emit_gathers, bar=bar_p1)
        emit_gathers(len(gjobs))
        gstg_ctx.close()
        if DEBUG_DUMP in (12, 121, 122):
            p.emit("vector", lambda e: e.engine_nop(),
                   waits=[qa_ready[QPN * 2 - 1], gst["done"]], inc=sVE)
            dump_and_finish(trunk_init)
            p.build(ctx)
            return nc

        bar3 = p.snap()
        ctxa_ready = {}
        halves_a = []
        for half in range(2):
            cols = slice(half * 512, half * 512 + 512)
            steps = [(SCHED_A[half][s], MSLOT_A[half][s])
                     for s in range(len(SCHED_A[half]))]
            halves_a.append((cols, steps))

        from contextlib import ExitStack as _ES
        with _ES() as p3ctx:
            _e = p3ctx.enter_context
            MSK = _e(nc.sbuf_tensor("a_msk", [128, 4, 512], BF16))
            MS = _e(nc.sbuf_tensor("a_ms", [128, 8], F32))
            CSF = _e(nc.sbuf_tensor("a_cs", [128, 2, SEQ], BF16))
            WKV = _e(nc.sbuf_tensor("a_wk", [128, NCHUNK, 256], BF16))
            XF = _e(nc.sbuf_tensor("a_xf", [128, 2, SEQ], BF16))
            KBR = _e(nc.sbuf_tensor("a_kb", [128, SEQ], BF16))
            VTD = KBR
            AT1 = _e(nc.sbuf_tensor("a_t1", [128, SEQ], BF16))
            KT = _e(nc.sbuf_tensor("a_kt", [128, SEQ], BF16))
            VG = _e(nc.sbuf_tensor("a_vg", [128, 16, 128], BF16))
            AQB = _e(nc.sbuf_tensor("a_qb", [128, 2, TOK], BF16))
            APB = _e(nc.sbuf_tensor("a_p", [128, 4, 512], BF16))
            ADA = _e(nc.sbuf_tensor("a_da", [128, 2, 512], F32))
            DHL = _e(nc.sbuf_tensor("a_dhl", [128, 2, 512], BF16))
            RHI = _e(nc.sbuf_tensor("a_rhi", [1, 512], BF16))
            RLO = _e(nc.sbuf_tensor("a_rlo", [1, 512], BF16))
            ADRS = _e(nc.sbuf_tensor("a_drs", [128, TOK], F32))
            ACX = _e(nc.sbuf_tensor("a_cx", [128, 2, TOK], BF16))

            p.dma("sync", MSK[:], masksD.rearrange("m p q -> p m q"),
                  waits=[gst["done"], bar3])
            tmsk = p.dma("sync", MS[:], mscalD, waits=[bar3])
            tcsf = p.dma("sync", CSF[:, 0, :], cosf, waits=[bar3])
            tsnf = p.dma("sync", CSF[:, 1, :], sinf, waits=[bar3])

            fs = {"qbuf": AQB, "qfree": [bar3, bar3], "sc_ps": None,
                  "sc_free": [bar3, bar3, bar3, bar3], "p_sb": APB,
                  "p_free": [[bar3], [bar3], [bar3], [bar3]], "pi": 0,
                  "dacc": ADA, "dacc_free": [bar3, bar3], "dn_ps": None,
                  "dn_free": bar3, "dr_sb": ADRS, "dr_sb_free": bar3,
                  "dhl": DHL, "dhl_free": [bar3, bar3],
                  "rr": [ADRS[0:1, :], RHI, RLO], "rr_free": bar3,
                  "ctx_ps": None, "ctx_sb": ACX, "co_free": [bar3, bar3],
                  "ctx_gate": bar3}

            xf_free = [bar3, bar3]
            wkv_free = bar3
            kv_sbuf_gate = bar3
            APS = PS0[:, :]
            BPS = PS1[:, :]
            for g in range(NKV):
                twk = p.dma("gpsimd", WKV[:, :, 0:128],
                            wkT[:, g * 128:(g + 1) * 128].rearrange(
                                "(c p) w -> p c w", p=128),
                            waits=[wkv_free])
                twv = p.dma("gpsimd", WKV[:, :, 128:256],
                            wvT[:, g * 128:(g + 1) * 128].rearrange(
                                "(c p) w -> p c w", p=128),
                            waits=[wkv_free])
                tmmk = tmmv = None
                for c in range(NCHUNK):
                    sl = c % 2
                    tl = p.dma("sync" if c % 2 == 0 else "scalar",
                               XF[:, sl, :],
                               xn1[c * 128:(c + 1) * 128, :],
                               waits=[xn1_done, xf_free[sl]])
                    for j in range(4):
                        s5 = slice(j * 512, (j + 1) * 512)
                        tmmk = p.emit("tensor", lambda e, c=c, sl=sl, s5=s5: e.matmul(
                            APS[:, s5], WKV[:, c, 0:128], XF[:, sl, s5],
                            start=(c == 0), stop=(c == NCHUNK - 1)),
                            waits=[tl, twk, twv, kv_sbuf_gate], inc=sPE)
                        tmmv = p.emit("tensor", lambda e, c=c, sl=sl, s5=s5: e.matmul(
                            BPS[:, s5], WKV[:, c, 128:256], XF[:, sl, s5],
                            start=(c == 0), stop=(c == NCHUNK - 1)),
                            waits=[tl], inc=sPE)
                    xf_free[sl] = tmmv
                wkv_free = tmmv
                tke = p.emit("scalar", lambda e: e.copy(KBR[:], APS[:]),
                             waits=[tmmk, kv_sbuf_gate], inc=sAC)
                tv1 = p.emit("vector", lambda e: e.tensor_tensor(
                    AT1[:], KBR[:], CSF[:, 0, :], ALU.mult),
                    waits=[tke, tcsf, kv_sbuf_gate], inc=sVE)
                trm = None
                for j in range(4):
                    s5 = slice(j * 512, (j + 1) * 512)
                    trm = p.emit("tensor", lambda e, s5=s5: e.matmul(
                        APS[:, s5], ROT[:], KBR[:, s5], start=True,
                        stop=True), waits=[tke, tv1], inc=sPE)
                tve = p.emit("scalar", lambda e: e.copy(VTD[:], BPS[:]),
                             waits=[tmmv, trm, kv_sbuf_gate], inc=sAC)
                tv2 = p.emit("vector", lambda e: e.tensor_tensor(
                    KT[:], APS[:], CSF[:, 1, :], ALU.mult),
                    waits=[trm, tsnf, kv_sbuf_gate], inc=sVE)
                tkt = p.emit("vector", lambda e: e.tensor_tensor(
                    KT[:], KT[:], AT1[:], ALU.add),
                    waits=[tv2, tv1], inc=sVE)
                tvg = []
                for j in range(16):
                    tvg.append(p.dma_emit(
                        "sync",
                        lambda e, j=j: e.dma_start_transpose(
                            VG[:, j, :], VTD[:, j * 128:(j + 1) * 128]),
                        waits=[tve, kv_sbuf_gate]))
                fs["sc_ps"] = [APS[:, 0:512], APS[:, 512:1024],
                               BPS[:, 1024:1536], BPS[:, 1536:2048]]
                fs["sc_pair"] = [APS[:, 0:1024], BPS[:, 1024:2048]]
                fs["p_pair"] = [
                    APB[:, 0:2, :].rearrange("p a b -> p (a b)"),
                    APB[:, 2:4, :].rearrange("p a b -> p (a b)")]
                fs["ctx_ps"] = BPS[:, 0:TOK]
                fs["dn_ps"] = APS[:, TOK:SEQ]
                kv_sbuf_gate = flash(
                    fs, [4 * g + i for i in range(4)],
                    lambda t: KT[:, t * 128:(t + 1) * 128],
                    lambda t: VG[:, t, :], qa_sp, qa_ready,
                    [tkt, tke, tv2, tve], tvg, halves_a, (MSK, MS), ctx_sp,
                    ctxa_ready, t_msk=tmsk)

        if DEBUG_DUMP == 13:
            p.emit("vector", lambda e: e.engine_nop(),
                   waits=[ctxa_ready[NH - 1]], inc=sVE)
            dump_and_finish(trunk_init)
            p.build(ctx)
            return nc
        bar4 = p.snap()
        res1 = wo_residual("woa", woaT, ctxa_ready, {}, bar=bar4)

        if DEBUG_DUMP == 1:
            dump_and_finish(res1)
            p.build(ctx)
            return nc


        def ffn(tag, trunk_gate, bar=None):
            res = {}
            bar = bar or []
            from contextlib import ExitStack as _ES
            with _ES() as ffctx:
                _e = ffctx.enter_context
                ATS = _e(nc.sbuf_tensor(f"{tag}_at", [128, NCHUNK, 128], BF16))
                BBS = _e(nc.sbuf_tensor(f"{tag}_bb", [128, NCHUNK, 128], BF16))
                WRS = _e(nc.sbuf_tensor(f"{tag}_wr", [128, NCHUNK, 8], BF16))
                t_at = p.dma("gpsimd", ATS[:],
                             aT.rearrange("(c p) w -> p c w", p=128),
                             waits=[bar])
                t_bb = p.dma("gpsimd", BBS[:],
                             bbig.rearrange("p (c w) -> p c w", w=128),
                             waits=[bar])
                t_wr = p.dma("gpsimd", WRS[:],
                             wrT.rearrange("(c p) w -> p c w", p=128),
                             waits=[bar])
                RR = _e(nc.sbuf_tensor(f"{tag}_rr", [8, TOK], F32))
                RTOK = _e(nc.sbuf_tensor(f"{tag}_rt", [128, 8, 8], F32))
                MX = _e(nc.sbuf_tensor(f"{tag}_mx", [128, 8, 1], F32))
                EXT = _e(nc.sbuf_tensor(f"{tag}_ex", [128, 8, 8], F32))
                SM = _e(nc.sbuf_tensor(f"{tag}_sm", [128, 8, 1], F32))
                STOK = _e(nc.sbuf_tensor(f"{tag}_sb", [128, 8, 8], F32))
                STT = _e(nc.sbuf_tensor(f"{tag}_st", [8, TOK], BF16))
                SREP = _e(nc.sbuf_tensor(f"{tag}_sr", [128, TOK], F32))
                LOWB = _e(nc.sbuf_tensor(f"{tag}_lo", [128, TOK], BF16))
                psA = PS0[:, 0:TOK]
                psC = PS1[:, 0:TOK]

                fin_tok = []

                def consume(c, xn_ap, tn):
                    last = None
                    for j in range(2):
                        s5 = slice(j * 512, (j + 1) * 512)
                        p.emit("tensor", lambda e, c=c, s5=s5, xn_ap=xn_ap: e.matmul(
                            psC[0:8, s5], WRS[:, c, :], xn_ap[:, s5],
                            start=(c == 0), stop=(c == NCHUNK - 1)),
                            waits=[tn, t_wr, bar], inc=sPE)
                        last = p.emit("tensor", lambda e, c=c, s5=s5, xn_ap=xn_ap: e.matmul(
                            psA[:, s5], ATS[:, c, :], xn_ap[:, s5],
                            start=(c == 0), stop=(c == NCHUNK - 1)),
                            waits=[t_at], inc=sPE)
                    if c == NCHUNK - 1:
                        fin_tok.append(last)
                    return last

                ln_trunk(tag + "l", trunk_gate, consume, bar=bar)
                psB = PS0[:, TOK:SEQ]
                psD = PS1[:, TOK:SEQ]
                t_rl = fin_tok[0]
                t_rr = p.emit("vector", lambda e: e.tensor_scalar(
                    RR[:], psC[0:8, :], BRS[:], None, ALU.add),
                    waits=[t_rl, t_const], inc=sVE)
                tp_prev = None
                tp_toks = []
                for j in range(8):
                    tt = p.emit("tensor", lambda e, j=j: e.transpose(
                        psB[:, j * 8:(j + 1) * 8],
                        RR[:, j * 128:(j + 1) * 128], IDF[0:8, 0:8]),
                        waits=[t_rr, t_const, tp_prev], inc=sPE)
                    te = p.emit("scalar", lambda e, j=j: e.copy(
                        RTOK[:, j, :], psB[:, j * 8:(j + 1) * 8]), waits=[tt],
                        inc=sAC)
                    tp_prev = te
                    tp_toks.append(te)
                sm_toks = []
                for j in range(8):
                    tmx = p.emit("vector", lambda e, j=j: e.tensor_reduce(
                        MX[:, j, :], RTOK[:, j, :], AX.X, ALU.max, negate=True),
                        waits=[tp_toks[j]], inc=sVE)
                    tex = p.emit("scalar", lambda e, j=j: e.activation(
                        EXT[:, j, :], RTOK[:, j, :], AF.Exp, bias=MX[:, j, :]),
                        waits=[tmx], inc=sAC)
                    tsm = p.emit("vector", lambda e, j=j: e.tensor_reduce(
                        SM[:, j, :], EXT[:, j, :], AX.X, ALU.add),
                        waits=[tex], inc=sVE)
                    trc = p.emit("vector", lambda e, j=j: e.reciprocal(
                        SM[:, j, :], SM[:, j, :]), waits=[tsm], inc=sVE)
                    tmu = p.emit("vector", lambda e, j=j: e.tensor_scalar(
                        STOK[:, j, :], EXT[:, j, :], SM[:, j, :], None,
                        ALU.mult), waits=[trc], inc=sVE)
                    sm_toks.append(tmu)
                tb_prev = tp_prev
                tst = None
                for j in range(8):
                    tt = p.emit("tensor", lambda e, j=j: e.transpose(
                        psB[0:8, 512:640], STOK[:, j, :], IDF[:]),
                        waits=[sm_toks[j], tb_prev], inc=sPE)
                    tst = p.emit("scalar", lambda e, j=j: e.copy(
                        STT[:, j * 128:(j + 1) * 128], psB[0:8, 512:640]),
                        waits=[tt], inc=sAC)
                    tb_prev = tst
                tsr = None
                for j in range(2):
                    s5 = slice(j * 512, (j + 1) * 512)
                    tsr = p.emit("tensor", lambda e, s5=s5: e.matmul(
                        psB[:, s5], E8S[:], STT[:, s5], start=True, stop=True),
                        waits=[tst, t_const], inc=sPE)
                tse = p.emit("scalar", lambda e: e.copy(SREP[:], psB),
                             waits=[tsr], inc=sAC)
                tlo = p.emit("vector", lambda e: e.tensor_tensor(
                    LOWB[:], psA, SREP[:], ALU.mult),
                    waits=[fin_tok[0], tse], inc=sVE)
                pps = [psC, psD]
                pp_free = [t_rr, None]
                for oc in range(NCHUNK):
                    sl = oc % 2
                    tmm = None
                    for j in range(2):
                        s5 = slice(j * 512, (j + 1) * 512)
                        tmm = p.emit("tensor", lambda e, oc=oc, s5=s5, sl=sl: e.matmul(
                            pps[sl][:, s5], BBS[:, oc, :], LOWB[:, s5],
                            start=True, stop=True),
                            waits=[tlo, t_bb, pp_free[sl]], inc=sPE)
                    tev = p.emit("vector", lambda e, oc=oc, sl=sl: e.tensor_tensor(
                        TR[:, oc, :], TR[:, oc, :], pps[sl], ALU.add),
                        waits=[tmm, trunk_gate.get(oc)], inc=sVE)
                    pp_free[sl] = tev
                    res[oc] = tev
            return res

        bar5 = p.snap()
        res2 = ffn("f2", res1, bar=bar5)
        if DEBUG_DUMP == 2:
            dump_and_finish(res2)
            p.build(ctx)
            return nc

        xn3_state = {"t": None}

        def consume3(c, xn_ap, tn):
            ts = p.dma("scalar", xn3[c * 128:(c + 1) * 128, :], xn_ap,
                       waits=[tn])
            xn3_state["t"] = ts
            return ts

        bar6 = p.snap()
        ln_trunk("l3", res2, consume3, bar=bar6)
        bar6b = p.snap()
        qb_ready = q_production("qb", wqb, xn3, qb_sp, xn3_state["t"],
                                bar=bar6b)

        bar7 = p.snap()
        ctxb_ready = {}
        halves_b = [(slice(0, 512), [(t, None) for t in range(64)]),
                    (slice(512, TOK), [(t, None) for t in range(64)])]
        from contextlib import ExitStack as _ES2
        with _ES2() as p7ctx:
            _e = p7ctx.enter_context
            BKT = _e(nc.sbuf_tensor("b_kt", [128, 64, 128], BF16))
            BVG = _e(nc.sbuf_tensor("b_vg", [128, 64, 128], BF16))
            BQB = _e(nc.sbuf_tensor("b_qb", [128, 2, TOK], BF16))
            BPB = _e(nc.sbuf_tensor("b_p", [128, 4, 512], BF16))
            BDA = _e(nc.sbuf_tensor("b_da", [128, 2, 512], F32))
            BHL = _e(nc.sbuf_tensor("b_dhl", [128, 2, 512], BF16))
            BRHI = _e(nc.sbuf_tensor("b_rhi", [1, 512], BF16))
            BRLO = _e(nc.sbuf_tensor("b_rlo", [1, 512], BF16))
            BDRS = _e(nc.sbuf_tensor("b_drs", [128, TOK], F32))
            BCX = _e(nc.sbuf_tensor("b_cx", [128, 2, TOK], BF16))
            bslots = [PS0[:, j * 512:(j + 1) * 512] for j in range(4)]
            bpairs = [PS0[:, 0:1024], PS0[:, 1024:2048]]
            bcps = PS1[:, 0:TOK]
            bdnp = PS1[:, TOK:SEQ]

            fsb = {"qbuf": BQB, "qfree": [bar7, bar7], "sc_ps": bslots,
                   "sc_free": [bar7] * 4, "p_sb": BPB,
                   "p_free": [[bar7], [bar7], [bar7], [bar7]],
                   "pi": 0, "dacc": BDA, "dacc_free": [bar7, bar7],
                   "dn_ps": bdnp, "dn_free": bar7, "dr_sb": BDRS,
                   "dr_sb_free": bar7, "dhl": BHL, "dhl_free": [bar7, bar7],
                   "rr": [BDRS[0:1, :], BRHI, BRLO], "rr_free": bar7,
                   "sc_pair": bpairs,
                   "p_pair": [BPB[:, 0:2, :].rearrange("p a b -> p (a b)"),
                              BPB[:, 2:4, :].rearrange("p a b -> p (a b)")],
                   "ctx_ps": bcps, "ctx_sb": BCX,
                   "co_free": [bar7, bar7], "ctx_gate": bar7}

            kv_gate = bar7
            for g in range(NKV):
                tk = p.dma_emit("sync", lambda e, g=g: e.dma_start_transpose(
                    BKT[:].rearrange("p c w -> p (c w)"),
                    kvg[0, g].rearrange("a b -> (a b)").rearrange(
                        "(t d) -> t d", d=128)),
                    waits=[gst["done"], kv_gate])
                tv = p.dma("sync", BVG[:],
                           kvg[1, g].rearrange("(c b) (z d) -> (b z) c d",
                                               b=8, d=128),
                           waits=[gst["done"], kv_gate])
                kv_gate = flash(
                    fsb, [4 * g + i for i in range(4)],
                    lambda t: BKT[:, t, :], lambda t: BVG[:, t, :],
                    qb_sp, qb_ready, [tk, tv], [tk, tv], halves_b, None,
                    ctx_sp, ctxb_ready)

        bar8 = p.snap()
        res3 = wo_residual("wob", wob, ctxb_ready, res2, bar=bar8)
        if DEBUG_DUMP == 3:
            dump_and_finish(res3)
            p.build(ctx)
            return nc

        bar9 = p.snap()
        res4 = ffn("f4", res3, bar=bar9)

        dump_and_finish(res4)
        p.build(ctx)
    return nc


def _host_tables():
    inv = 1.0 / (10000.0 ** (np.arange(0, HD, 2, dtype=np.float64) / HD))
    pos = np.arange(SEQ, dtype=np.float64)
    ang = np.outer(inv, pos)
    emb = np.concatenate([ang, ang], axis=0)
    return np.cos(emb), np.sin(emb)


def _core_positions(core):
    b = core // 2
    if core % 2 == 0:
        own = np.r_[0:512, 1536:2048]
    else:
        own = np.r_[512:1536]
    oth = np.setdiff1d(np.arange(SEQ), own)
    return b, own, oth


def _tri_masks():
    m = np.zeros((4, 128, 512), dtype=np.float32)
    for j in range(4):
        m[j] = (j * 128 + np.arange(128)[:, None] <= np.arange(512)[None, :])
    return m.astype(ml_dtypes.bfloat16)


def _mscal_for(core):
    v = np.zeros(8, dtype=np.float32)
    v[0:4] = 0.0 if core % 2 == 0 else 1.0
    v[4:8] = 1.0 if core % 2 == 0 else 0.0
    return np.broadcast_to(v, (128, 8)).copy()


def _check_sched(core, ownpos, locpos):
    tri = _tri_masks().astype(np.float32)
    ms = _mscal_for(core)[0]
    for half in range(2):
        qpos = ownpos[half * 512:(half + 1) * 512]
        cover = np.zeros((SEQ, 512), dtype=np.float64)
        for s, tile in enumerate(SCHED_A[half]):
            slot = MSLOT_A[half][s]
            tloc = np.arange(tile * 128, (tile + 1) * 128)
            if slot is None:
                keep = np.ones((128, 512))
            elif slot[0] == "tri":
                keep = tri[slot[1]]
            else:
                keep = np.full((128, 512), ms[slot[1]])
            cover[tloc[:, None], np.arange(512)[None, :]] += keep
        want = (locpos[:, None] <= qpos[None, :]).astype(np.float64)
        assert (cover == want).all(), f"flash-a schedule mismatch core {core}"


_NC_CACHE = {}


def kernel(L0_attn_a_hidden_states, L0_attn_a_Wq, L0_attn_a_Wk, L0_attn_a_Wv,
           L0_attn_a_Wo, L0_attn_b_Wq, L0_attn_b_Wo, L0_attn_b_kv_cache,
           L0_attn_b_block_table, L0_ffn_A, L0_ffn_B, L0_ffn_Wr, L0_ffn_br):
    x = np.asarray(L0_attn_a_hidden_states, dtype=np.float32)
    f32 = lambda a: np.ascontiguousarray(np.asarray(a, dtype=np.float32))
    consts = {
        "wqaT": f32(np.asarray(L0_attn_a_Wq).T),
        "wkT": f32(np.asarray(L0_attn_a_Wk).T),
        "wvT": f32(np.asarray(L0_attn_a_Wv).T),
        "woaT": f32(np.asarray(L0_attn_a_Wo).T),
        "wqb": f32(L0_attn_b_Wq),
        "wob": f32(L0_attn_b_Wo),
        "aT": f32(np.asarray(L0_ffn_A).reshape(128, D).T),
        "bbig": f32(np.asarray(L0_ffn_B).transpose(0, 2, 1).reshape(128, D)),
        "wrT": f32(np.asarray(L0_ffn_Wr).T),
        "br": f32(np.asarray(L0_ffn_br).reshape(8, 1)),
        "kvc": f32(np.asarray(L0_attn_b_kv_cache)
                   .transpose(1, 2, 0, 3, 4).reshape(2, NKV, NBLK, BLKSZ * HD)),
        "identF": np.eye(128, dtype=np.float32),
        "identB": np.eye(128, dtype=ml_dtypes.bfloat16),
        "onesB": np.ones((128, 1), dtype=ml_dtypes.bfloat16),
        "onesRF": np.ones((1, 128), dtype=np.float32),
        "onesRB": np.ones((1, 128), dtype=ml_dtypes.bfloat16),
    }
    bt = np.asarray(L0_attn_b_block_table, dtype=np.int32)
    consts["btc"] = np.ascontiguousarray(bt.reshape(4, 128).T).view(np.uint32)
    consts["e8"] = (np.kron(np.eye(8), np.ones((1, 16))) * 16.0).astype(
        ml_dtypes.bfloat16)
    rotTm = np.zeros((128, 128), dtype=np.float32)
    for i in range(64):
        rotTm[64 + i, i] = -1.0
        rotTm[i, 64 + i] = 1.0
    consts["rotT"] = rotTm.astype(ml_dtypes.bfloat16)

    cosg, sing = _host_tables()
    in_maps = []
    metas = []
    for core in range(8):
        b, own, oth = _core_positions(core)
        loc = np.concatenate([own, oth])
        if core < 2:
            _check_sched(core, own, loc)
        m = dict(consts)
        m["xT"] = np.ascontiguousarray(x[b].T[:, loc])
        m["cosf"] = np.ascontiguousarray(cosg[:, loc]).astype(ml_dtypes.bfloat16)
        m["sinf"] = np.ascontiguousarray(sing[:, loc]).astype(ml_dtypes.bfloat16)
        m["coso"] = np.ascontiguousarray(cosg[:, own]).astype(ml_dtypes.bfloat16)
        m["sino"] = np.ascontiguousarray(sing[:, own]).astype(ml_dtypes.bfloat16)
        m["masks"] = _tri_masks()
        m["mscal"] = _mscal_for(core)
        in_maps.append(m)
        metas.append((b, own))

    if "nc" not in _NC_CACHE:
        _NC_CACHE["nc"] = build_program()
    nc = _NC_CACHE["nc"]
    res = run_bass_kernel_spmd(nc, in_maps, core_ids=list(range(8)))

    out = np.zeros((4, SEQ, D), dtype=np.float32)
    key = "dbg" if DEBUG_DUMP else "outT"
    for core in range(8):
        b, own = metas[core]
        out[b, own, :] = res.results[core][key].T
    return out
